# version 18
# baseline (speedup 1.0000x reference)
import numpy as np
from contextlib import ExitStack

import concourse.bass as bass
import concourse.mybir as mybir
from concourse.bass_utils import run_bass_kernel_spmd

F32 = mybir.dt.float32
BF16 = mybir.dt.bfloat16
I32 = mybir.dt.int32
AF = mybir.ActivationFunctionType
ALU = mybir.AluOpType
AX = mybir.AxisListType

D = 2048
NC = 16
T = 1024
E = 17
FC = 4
QKV = 6144
MG = 48
ISQ = float(1.0 / np.sqrt(2048.0))
EPS = 1e-5
GROUPS = [list(range(0, 6)), list(range(6, 12)), list(range(12, 17))]
RG = [[0, 1], [2, 3], [4, 5], [6, 7]]


def build_nc(debug=False):
    nc = bass.Bass("TRN2", target_bir_lowering=False, detect_race_conditions=False)

    i_x = nc.dram_tensor("i_x", [T, D], F32, kind="ExternalInput").ap()
    i_wqkvT = nc.dram_tensor("i_wqkvT", [MG, 128, D], F32, kind="ExternalInput").ap()
    i_woutT = nc.dram_tensor("i_woutT", [NC, 128, D], F32, kind="ExternalInput").ap()
    i_g17 = nc.dram_tensor("i_g17", [128, NC, 18], F32, kind="ExternalInput").ap()
    i_bq = nc.dram_tensor("i_bq", [128, MG], F32, kind="ExternalInput").ap()
    i_bo = nc.dram_tensor("i_bo", [128, NC], F32, kind="ExternalInput").ap()
    i_wg = nc.dram_tensor("i_wg", [E, FC, 128, D], F32, kind="ExternalInput").ap()
    i_wu = nc.dram_tensor("i_wu", [E, FC, 128, D], F32, kind="ExternalInput").ap()
    i_wd = nc.dram_tensor("i_wd", [NC, 128, E * 512], F32, kind="ExternalInput").ap()
    i_hoff = nc.dram_tensor("i_hoff", [1, 1], I32, kind="ExternalInput").ap()
    o_y = nc.dram_tensor("o_y", [T, D], F32, kind="ExternalOutput").ap()
    dbg = {}
    if debug:
        for nm in ("d_x1", "d_x2"):
            dbg[nm] = nc.dram_tensor(nm, [128, NC * T], F32, kind="ExternalOutput").ap()
        dbg["d_x0"] = nc.dram_tensor("d_x0", [128, NC * T], F32, kind="ExternalOutput").ap()
        dbg["d_y1"] = nc.dram_tensor("d_y1", [128, NC * T], BF16, kind="ExternalOutput").ap()
        dbg["d_comb"] = nc.dram_tensor("d_comb", [32, T], BF16, kind="ExternalOutput").ap()
        dbg["d_rt"] = nc.dram_tensor("d_rt", [32, T], F32, kind="ExternalOutput").ap()
        dbg["d_l17"] = nc.dram_tensor("d_l17", [128, 256], F32, kind="ExternalOutput").ap()
        dbg["d_m8"] = nc.dram_tensor("d_m8", [128, 64], F32, kind="ExternalOutput").ap()
        dbg["d_mask"] = nc.dram_tensor("d_mask", [128, 128], F32, kind="ExternalOutput").ap()
        dbg["d_me"] = nc.dram_tensor("d_me", [128, 128], F32, kind="ExternalOutput").ap()
        dbg["d_c17"] = nc.dram_tensor("d_c17", [128, 256], F32, kind="ExternalOutput").ap()
        dbg["d_nmv"] = nc.dram_tensor("d_nmv", [128, 8], F32, kind="ExternalOutput").ap()
        dbg["d_nma"] = nc.dram_tensor("d_nma", [128, 8], F32, kind="ExternalOutput").ap()

    cc_in_q = [nc.dram_tensor("cc_in0", [2048, 512], BF16),
               nc.dram_tensor("cc_in1", [2048, 512], BF16)]
    cc_out_q = [nc.dram_tensor("cc_out0", [4096, 512], BF16),
                nc.dram_tensor("cc_out1", [4096, 512], BF16)]

    ENGS = ("tensor", "vector", "scalar", "gpsimd", "sync")
    Q = {en: [] for en in ENGS}
    CNT = {s: 0 for s in ("pe", "dv", "ac", "pl", "cc", "small", "send", "rb", "pc", "xs0", "xs1", "wr0", "wr1", "wr2", "wd0", "wd1")}
    LASTW = {en: {} for en in ENGS}

    def inc(s, n=1):
        CNT[s] += n
        return CNT[s]

    def W(eng, s, val):
        if val <= 0:
            return
        if LASTW[eng].get(s, 0) >= val:
            return
        LASTW[eng][s] = val
        Q[eng].append(("W", s, val))

    def EM(eng, fn):
        Q[eng].append(fn)

    with ExitStack() as st:
        en_ = st.enter_context
        XT = en_(nc.sbuf_tensor("XT", [128, NC * T], F32))
        YT = en_(nc.sbuf_tensor("YT", [128, NC * T], BF16))
        AR1 = en_(nc.sbuf_tensor("AR1", [128, 24576], BF16))
        AR2 = en_(nc.sbuf_tensor("AR2", [128, 16384], BF16))
        WR = en_(nc.sbuf_tensor("WR", [128, 3 * 2048], BF16))
        MISC = en_(nc.sbuf_tensor("MISC", [128, 4096], BF16))
        g17s = en_(nc.sbuf_tensor("g17s", [128, NC * 18], F32))
        bq_sb = en_(nc.sbuf_tensor("bq_sb", [128, MG], F32))
        bo_sb = en_(nc.sbuf_tensor("bo_sb", [128, NC], F32))
        identB = en_(nc.sbuf_tensor("identB", [128, 128], BF16))
        identF = en_(nc.sbuf_tensor("identF", [128, 128], F32))
        ones1f = en_(nc.sbuf_tensor("ones1f", [1, 128], F32))
        onescolB = en_(nc.sbuf_tensor("onescolB", [128, 1], BF16))
        ones128B = en_(nc.sbuf_tensor("ones128B", [128, 128], BF16))
        onescolF = en_(nc.sbuf_tensor("onescolF", [128, 1], F32))
        u17 = en_(nc.sbuf_tensor("u17", [1, 18], F32))
        Srow = en_(nc.sbuf_tensor("Srow", [1, 18], F32))
        silscr = en_(nc.sbuf_tensor("silscr", [128, 512], F32))
        v4sb = en_(nc.sbuf_tensor("v4sb", [128, 2 * 128], BF16))
        sbao = en_(nc.sbuf_tensor("sbao", [128, 2 * 128], BF16))
        Rsb = en_(nc.sbuf_tensor("Rsb", [128, 2 * 32], F32))
        Msc = en_(nc.sbuf_tensor("Msc", [128, 2 * 32], F32))
        hoffs = en_(nc.sbuf_tensor("hoffs", [1, 1], I32))
        nmsnapv = en_(nc.sbuf_tensor("nmsnapv", [128, 8], F32))
        nmsnapa = en_(nc.sbuf_tensor("nmsnapa", [128, 8], F32))
        SEL = en_(nc.sbuf_tensor("SEL", [32, E * 128], BF16))
        PS = en_(nc.psum_tensor("PS", [128, 4096], F32))
        sems = {s: en_(nc.semaphore("s_" + s)) for s in CNT}
        blk = en_(nc.Block())

        def EMI(eng, sn, fn):
            def wrapped(engine, fn=fn, sn=sn):
                r = fn(engine)
                r.then_inc(sems[sn], 1)
            Q[eng].append(wrapped)
            return inc(sn)

        def bk(b, off=0, w=512):
            return PS[:, b * 512 + off:b * 512 + off + w]

        def bkp(b, p0, pn, off=0, w=512):
            return PS[p0:p0 + pn, b * 512 + off:b * 512 + off + w]

        def bkb16(b):
            return PS[:, b * 512:(b + 1) * 512].bitcast(BF16)

        bank_guard = {b: ("dv", 0) for b in range(8)}

        def pe_bank(b):
            s, c = bank_guard[b]
            W("tensor", s, c)

        def hard_sync():
            for eng in ("tensor", "vector", "scalar"):
                for s in ("pe", "dv", "ac"):
                    W(eng, s, CNT[s])

        def ar1b(o, n):
            return AR1[:, o:o + n]

        def ar1f(o, n):
            return AR1[:, o:o + 2 * n].bitcast(F32)

        def ar2b(o, n):
            return AR2[:, o:o + n]

        def ar2f(o, n):
            return AR2[:, o:o + 2 * n].bitcast(F32)

        qkvT = ar1b(0, MG * 512)
        qkv3 = qkvT.rearrange("p (mg t) -> p mg t", t=512)

        def a_sl(k, fc, ch):
            o = (k * FC + fc) * T + ch * 512
            return ar1b(o, 512)

        xstage = [ar1f(0, 2048), ar1f(4096, 2048)]
        wd_slot = [ar2b(0, 3072), ar2b(3072, 3072)]
        cb_slot = [ar2b(6144, 1024), ar2b(7168, 1024)]
        rt_sb = AR2[0:32, 8192:10240].bitcast(F32)
        combT = AR2[0:32, 10240:11264]
        c17a = ar2f(11264, 256)
        l17a = ar2f(11776, 256)
        expva = ar2f(12288, 128)
        maska = ar2f(12544, 128)
        m8a = ar2f(12800, 64)
        nm1a = ar2f(12928, 8)
        siga = ar2f(12944, 8)
        varrow = AR2[0:1, 13312:15360].bitcast(F32)
        xb16s = [ar2b(0, 1024), ar2b(2048, 1024)]
        sqbs = [ar2b(1024, 1024), ar2b(3072, 1024)]
        sendv = AR2
        STv = AR2

        v4b = silscr[:, 0:512].bitcast(BF16)
        negm_r = MISC[0:1, 0:2048].bitcast(F32)
        rstd_r = MISC[0:1, 2048:4096].bitcast(F32)
        expS0 = MISC[:, 0:1024]
        PTv0 = MISC[:, 1024:2048]
        scsub = MISC[:, 2048:3072].bitcast(F32)
        BD16 = MISC[:, 3072:4096]

        wr_state = {"i": 0, "guard": [0, 0, 0]}

        def wr_load(mk):
            sl = wr_state["i"] % 3
            sn = "wr%d" % sl
            W("gpsimd", "pe", wr_state["guard"][sl])
            slot_ap = WR[:, sl * 2048:(sl + 1) * 2048]
            EM("gpsimd", lambda g, mk=mk, a=slot_ap, sn=sn: mk(g, a).then_inc(sems[sn], 16))
            c = inc(sn, 16)
            wr_state["i"] += 1
            return sl, (sn, c)

        def wr_consumed(sl):
            wr_state["guard"][sl] = CNT["pe"]

        wd_state = {"i": 0, "guard": [0, 0]}

        def wd_load(mk):
            sl = wd_state["i"] % 2
            sn = "wd%d" % sl
            W("gpsimd", "dv", DBG_DV0)
            W("gpsimd", "pe", wd_state["guard"][sl])
            EM("gpsimd", lambda g, mk=mk, a=wd_slot[sl], sn=sn: mk(g, a).then_inc(sems[sn], 16))
            c = inc(sn, 16)
            wd_state["i"] += 1
            return sl, (sn, c)

        def wd_consumed(sl):
            wd_state["guard"][sl] = CNT["pe"]

        def mk_wquarter(base, ex, fc):
            def mk(g, slot_ap, base=base, ex=ex, fc=fc):
                return g.dma_start(out=slot_ap, in_=base[ex, fc, :, :])
            return mk

        def mk_wdslot(grp, dc):
            e0, e1 = grp[0], grp[-1] + 1
            def mk(g, slot_ap, e0=e0, e1=e1, dc=dc):
                ne = e1 - e0
                return g.dma_start(
                    out=slot_ap[:, 0:ne * FC * 128],
                    in_=i_wd[dc, :, e0 * 512:e1 * 512])
            return mk

        def mk_wcol(base, mg):
            def mk(g, slot_ap, base=base, mg=mg):
                return g.dma_start(out=slot_ap, in_=base[mg, :, :])
            return mk

        def f_init(g):
            g.memset(identB[:, :], 0.0)
            g.memset(identF[:, :], 0.0)
            g.memset(ones1f[:, :], 1.0)
            g.memset(onescolB[:, :], 1.0)
            g.memset(ones128B[:, :], 1.0)
            g.memset(onescolF[:, :], 1.0)
            g.memset(u17[:, :], 0.0)
            g.memset(SEL[:, :], 0.0).then_inc(sems["pl"], 1)
            g.wait_ge(sems["pl"], PL0)
            g.memset(u17[:, 17:18], 1.0)
            g.affine_select(out=identB[:, :], in_=identB[:, :], compare_op=ALU.not_equal,
                            fill=1.0, base=0, pattern=[[-1, 128]], channel_multiplier=1)
            g.affine_select(out=identF[:, :], in_=identF[:, :], compare_op=ALU.not_equal,
                            fill=1.0, base=0, pattern=[[-1, 128]], channel_multiplier=1)
            return g.affine_select(
                out=SEL.rearrange("p (e c) -> p e c", c=128),
                in_=SEL.rearrange("p (e c) -> p e c", c=128),
                compare_op=ALU.not_equal, fill=1.0, base=0,
                pattern=[[-1, E], [0, 128]], channel_multiplier=1)
        PL0 = CNT["pl"] + 1
        inc("pl")
        PL_INIT = EMI("gpsimd", "pl", f_init)

        def f_small(sy):
            sy.dma_start(out=g17s.rearrange("p (c e) -> p c e", e=18),
                         in_=i_g17).then_inc(sems["small"], 16)
            sy.dma_start(out=bq_sb[:, :], in_=i_bq).then_inc(sems["small"], 16)
            sy.dma_start(out=bo_sb[:, :], in_=i_bo).then_inc(sems["small"], 16)
            sy.dma_start(out=hoffs[:, :], in_=i_hoff).then_inc(sems["small"], 16)
        EM("sync", f_small)
        WS_SMALL = inc("small", 64)

        DBG_DV0 = EMI("vector", "dv", lambda v: v.memset(AR2[:, :], 0.0))

        W("tensor", "small", WS_SMALL)
        W("tensor", "pl", PL_INIT)

        def f_srow(t):
            for dc in range(NC):
                mm = t.matmul(bkp(7, 0, 1, 0, 18)[0:1, 0:18],
                              onescolF[:, :], g17s[:, dc * 18:(dc + 1) * 18],
                              start=(dc == 0), stop=(dc == NC - 1))
            return mm
        pc = EMI("tensor", "pe", f_srow)
        W("vector", "pe", pc)
        dvc = EMI("vector", "dv", lambda v: v.tensor_copy(Srow[0:1, 0:18],
                                                          bkp(7, 0, 1, 0, 18)[0:1, 0:18]))
        bank_guard[7] = ("dv", dvc)

        xst_guard = [0, 0]
        for tt in range(8):
            sl = tt % 2
            W("sync", "pe", xst_guard[sl])

            def fx(sy, tt=tt, sl=sl):
                sy.dma_start(out=xstage[sl][:, :],
                             in_=i_x[tt * 128:(tt + 1) * 128, :]).then_inc(sems["xs%d" % sl], 16)
            EM("sync", fx)
            wc = inc("xs%d" % sl, 16)
            W("tensor", "xs%d" % sl, wc)
            W("tensor", "pl", PL_INIT)
            for dc in range(NC):
                pb = dc % 2
                pe_bank(pb)
                pc = EMI("tensor", "pe", lambda t, dc=dc, pb=pb, sl=sl: t.transpose(
                    bk(pb, 0, 128).bitcast(F32)[:, 0:128],
                    xstage[sl][:, dc * 128:(dc + 1) * 128], identF[:, :]))
                W("vector", "pe", pc)
                dvc = EMI("vector", "dv", lambda v, tt=tt, dc=dc, pb=pb: v.tensor_copy(
                    XT[:, dc * T + tt * 128:dc * T + (tt + 1) * 128],
                    bk(pb, 0, 128).bitcast(F32)[:, 0:128]))
                bank_guard[pb] = ("dv", dvc)
            xst_guard[sl] = CNT["pe"]

        def ln_phase():
            W("vector", "pe", CNT["pe"])
            W("scalar", "pe", CNT["pe"])
            W("vector", "dv", CNT["dv"])
            W("scalar", "dv", CNT["dv"])
            for b in range(4, 8):
                pe_bank(b)
            pe_fm = {}
            for dc in range(NC):
                par = dc % 2
                if dc >= 2:
                    W("vector", "pe", pe_fm[dc - 2])
                    W("scalar", "pe", pe_fm[dc - 2])
                dvx = EMI("vector", "dv", lambda v, dc=dc, par=par: v.tensor_copy(
                    xb16s[par][:, :], XT[:, dc * T:(dc + 1) * T]))
                acs = EMI("scalar", "ac", lambda s, dc=dc, par=par: s.activation(
                    sqbs[par][:, :], XT[:, dc * T:(dc + 1) * T], AF.Square))
                W("tensor", "dv", dvx)
                W("tensor", "ac", acs)

                def fm(t, dc=dc, par=par):
                    for ch in range(2):
                        t.matmul(bk(4 + ch), ones128B[:, :],
                                 xb16s[par][:, ch * 512:(ch + 1) * 512],
                                 start=(dc == 0), stop=(dc == NC - 1))
                    for ch in range(2):
                        mm = t.matmul(bk(6 + ch), ones128B[:, :],
                                      sqbs[par][:, ch * 512:(ch + 1) * 512],
                                      start=(dc == 0), stop=(dc == NC - 1))
                    return mm
                pe_fm[dc] = EMI("tensor", "pe", fm)
            W("vector", "pe", pe_fm[NC - 1])

            def frow1(v):
                v.tensor_scalar(negm_r[0:1, 0:512], bkp(4, 0, 1), -1.0 / 2048.0,
                                None, op0=ALU.mult)
                v.tensor_scalar(negm_r[0:1, 512:1024], bkp(5, 0, 1), -1.0 / 2048.0,
                                None, op0=ALU.mult)
                v.tensor_scalar(varrow[0:1, 0:512], bkp(6, 0, 1), 1.0 / 2048.0,
                                EPS, op0=ALU.mult, op1=ALU.add)
                return v.tensor_scalar(varrow[0:1, 512:1024], bkp(7, 0, 1), 1.0 / 2048.0,
                                       EPS, op0=ALU.mult, op1=ALU.add)
            c_ = EMI("vector", "dv", frow1)
            for b in range(4, 8):
                bank_guard[b] = ("dv", c_)
            W("vector", "dv", c_)

            def frow2(v):
                v.tensor_tensor(rstd_r[0:1, :], negm_r[0:1, :], negm_r[0:1, :], op=ALU.mult)
                return v.tensor_tensor(varrow[0:1, :], varrow[0:1, :], rstd_r[0:1, :],
                                       op=ALU.subtract)
            dvrows = EMI("vector", "dv", frow2)
            W("scalar", "dv", dvrows)

            def frs(s):
                return s.activation(rstd_r[0:1, :], varrow[0:1, :], AF.Sqrt)
            acr = EMI("scalar", "ac", frs)
            W("vector", "ac", acr)
            dvrec = EMI("vector", "dv", lambda v: v.reciprocal(rstd_r[0:1, :],
                                                               rstd_r[0:1, :]))
            W("tensor", "dv", dvrec)
            for b in range(4, 8):
                pe_bank(b)

            def fbc(t):
                t.matmul(bk(4), ones1f[:, :], rstd_r[0:1, 0:512], start=True, stop=True)
                t.matmul(bk(5), ones1f[:, :], rstd_r[0:1, 512:1024], start=True, stop=True)
                t.matmul(bk(6), ones1f[:, :], negm_r[0:1, 0:512], start=True, stop=True)
                return t.matmul(bk(7), ones1f[:, :], negm_r[0:1, 512:1024], start=True, stop=True)
            pebc = EMI("tensor", "pe", fbc)
            W("vector", "pe", pebc)

            def fy(v):
                for dc in range(NC):
                    for ch in range(2):
                        sl = XT[:, dc * T + ch * 512:dc * T + ch * 512 + 512]
                        v.tensor_tensor(silscr[:, :], sl, bk(6 + ch), op=ALU.add)
                        mm = v.tensor_tensor(YT[:, dc * T + ch * 512:dc * T + ch * 512 + 512],
                                             silscr[:, :], bk(4 + ch), op=ALU.mult)
                return mm
            dvy = EMI("vector", "dv", fy)
            for b in range(4, 8):
                bank_guard[b] = ("dv", dvy)
            return dvy

        def router_phase(dv_xt=None):
            W("tensor", "dv", CNT["dv"] if dv_xt is None else dv_xt)
            pe_bank(0)
            pe_bank(1)

            def frt(t):
                for dc in range(NC):
                    for ch in range(2):
                        t.matmul(PS[0:18, ch * 512:(ch + 1) * 512],
                                 g17s[:, dc * 18:(dc + 1) * 18],
                                 XT[:, dc * T + ch * 512:dc * T + ch * 512 + 512],
                                 start=(dc == 0), stop=False)
                for ch in range(2):
                    t.matmul(PS[0:18, ch * 512:(ch + 1) * 512], u17[:, :],
                             rstd_r[0:1, ch * 512:(ch + 1) * 512], start=False, stop=False)
                t.matmul(PS[0:18, 0:512], Srow[0:1, 0:18],
                         negm_r[0:1, 0:512], start=False, stop=True)
                return t.matmul(PS[0:18, 512:1024], Srow[0:1, 0:18],
                                negm_r[0:1, 512:1024], start=False, stop=True)
            pert = EMI("tensor", "pe", frt)
            W("vector", "pe", pert)
            dvrt = EMI("vector", "dv", lambda v: v.tensor_copy(rt_sb[0:18, :], PS[0:18, 0:1024]))
            bank_guard[0] = ("dv", dvrt)
            bank_guard[1] = ("dv", dvrt)
            W("tensor", "dv", dvrt)
            pe_bank(2)
            pe_bank(3)

            def ftp(t):
                for tt in range(8):
                    b = 2 + tt // 4
                    off = (tt % 4) * 32
                    mm = t.transpose(bkp(b, 0, 128, off, 32)[:, 0:18],
                                     rt_sb[0:18, tt * 128:(tt + 1) * 128], identF[0:18, 0:18])
                return mm
            petp = EMI("tensor", "pe", ftp)
            W("vector", "pe", petp)

            def fl17a(v):
                for tt in range(8):
                    b = 2 + tt // 4
                    off = (tt % 4) * 32
                    tp = bkp(b, 0, 128, off, 32)
                    mm = v.tensor_scalar(l17a[:, tt * 32:tt * 32 + 17], tp[:, 0:17],
                                         tp[:, 17:18], None, op0=ALU.mult)
                return mm
            c_ = EMI("vector", "dv", fl17a)
            W("vector", "dv", c_)

            def fl17b(v):
                for tt in range(8):
                    mm = v.max(out=m8a[:, tt * 8:(tt + 1) * 8], in_=l17a[:, tt * 32:tt * 32 + 16])
                return mm
            c_ = EMI("vector", "dv", fl17b)
            W("vector", "dv", c_)

            def fl17(v):
                for tt in range(8):
                    v.tensor_scalar(nm1a[:, tt:tt + 1], m8a[:, tt * 8:tt * 8 + 1], -1.0,
                                    None, op0=ALU.mult)
                    v.tensor_scalar(maska[:, tt * 16:(tt + 1) * 16], l17a[:, tt * 32:tt * 32 + 16],
                                    m8a[:, tt * 8 + 3:tt * 8 + 4], None, op0=ALU.is_ge)
                return v.tensor_copy(siga[:, 0:8],
                                     l17a.rearrange("p (tt c) -> p tt c", c=32)[:, :, 16])
            dvl = EMI("vector", "dv", fl17)
            if debug and "nmsnap" not in dbg:
                dbg["nmsnap"] = True
                W("vector", "dv", dvl)
                EMI("vector", "dv", lambda v: v.tensor_copy(nmsnapv[:, :], nm1a[:, 0:8]))
                W("scalar", "dv", CNT["dv"])
                EMI("scalar", "ac", lambda s: s.copy(nmsnapa[:, :], nm1a[:, 0:8]))
                hard_sync()
            bank_guard[2] = ("dv", dvl)
            bank_guard[3] = ("dv", dvl)
            W("scalar", "dv", dvl)

            def fexp(s):
                for tt in range(8):
                    s.activation(expva[:, tt * 16:(tt + 1) * 16], l17a[:, tt * 32:tt * 32 + 16],
                                 AF.Exp, bias=nm1a[:, tt:tt + 1])
                s.drain()
                return s.activation(nm1a[:, 0:8], siga[:, 0:8], AF.Sigmoid)
            acx = EMI("scalar", "ac", fexp)
            W("vector", "ac", acx)

            def fcomb0(v):
                for tt in range(8):
                    mm = v.tensor_tensor(expva[:, tt * 16:(tt + 1) * 16],
                                         expva[:, tt * 16:(tt + 1) * 16],
                                         maska[:, tt * 16:(tt + 1) * 16], op=ALU.mult)
                return mm
            c_ = EMI("vector", "dv", fcomb0)
            W("vector", "dv", c_)

            def fcomb1(v):
                for tt in range(8):
                    mm = v.reduce_sum(m8a[:, tt * 8:tt * 8 + 1], expva[:, tt * 16:(tt + 1) * 16],
                                      axis=AX.X)
                return mm
            c_ = EMI("vector", "dv", fcomb1)
            W("vector", "dv", c_)

            def fcomb2(v):
                for tt in range(8):
                    mm = v.reciprocal(m8a[:, tt * 8 + 1:tt * 8 + 2], m8a[:, tt * 8:tt * 8 + 1])
                return mm
            c_ = EMI("vector", "dv", fcomb2)
            W("vector", "dv", c_)

            def fcomb(v):
                for tt in range(8):
                    v.tensor_scalar(c17a[:, tt * 32:tt * 32 + 16], expva[:, tt * 16:(tt + 1) * 16],
                                    m8a[:, tt * 8 + 1:tt * 8 + 2], 0.5, op0=ALU.mult, op1=ALU.mult)
                return v.tensor_scalar(c17a.rearrange("p (tt c) -> p tt c", c=32)[:, :, 16],
                                       nm1a[:, 0:8], 0.5, None, op0=ALU.mult)
            dvcb = EMI("vector", "dv", fcomb)
            W("tensor", "dv", dvcb)
            pe_bank(2)
            pe_bank(3)

            def fct(t):
                for tt in range(8):
                    b = 2 + tt // 4
                    off = (tt % 4) * 128
                    mm = t.transpose(bkp(b, 0, 32, off, 128)[0:17, :],
                                     c17a[:, tt * 32:tt * 32 + 17], identF[:, :])
                return mm
            pct = EMI("tensor", "pe", fct)
            W("vector", "pe", pct)

            def fce(v):
                v.memset(combT[0:32, :], 0.0)
                for tt in range(8):
                    b = 2 + tt // 4
                    off = (tt % 4) * 128
                    mm = v.tensor_copy(combT[0:17, tt * 128:(tt + 1) * 128],
                                       bkp(b, 0, 32, off, 128)[0:17, :])
                return mm
            dvct = EMI("vector", "dv", fce)
            bank_guard[2] = ("dv", dvct)
            bank_guard[3] = ("dv", dvct)
            return dvct

        def moe_phase(dv_combT, a_guard_pe0, dvy):
            dve_dc = {}
            cbg = [0, 0]
            a_guard_pe = a_guard_pe0
            prev_ascr_dv = 0
            first_gate = True
            for gi, grp in enumerate(GROUPS):
                G = len(grp)
                first_write = True
                for k, ex in enumerate(grp):
                    csl = k % 2

                    def emit_cb(ex=ex, csl=csl):
                        pe_bank(6)
                        pe_bank(7)
                        W("tensor", "dv", dv_combT)

                        def fcb(t, ex=ex):
                            t.matmul(bk(6), SEL[0:32, ex * 128:(ex + 1) * 128],
                                     combT[0:32, 0:512], start=True, stop=True)
                            return t.matmul(bk(7), SEL[0:32, ex * 128:(ex + 1) * 128],
                                            combT[0:32, 512:1024], start=True, stop=True)
                        pcb = EMI("tensor", "pe", fcb)
                        W("scalar", "pe", pcb)
                        W("scalar", "dv", cbg[csl])

                        def fcbe(s, csl=csl):
                            s.copy(cb_slot[csl][:, 0:512], bk(6))
                            return s.copy(cb_slot[csl][:, 512:1024], bk(7))
                        accb = EMI("scalar", "ac", fcbe)
                        bank_guard[6] = ("ac", accb)
                        bank_guard[7] = ("ac", accb)
                        return accb

                    if not first_gate:
                        accb = emit_cb()

                    for fc in range(FC):
                        slg, cg = wr_load(mk_wquarter(i_wg, ex, fc))
                        pcg = {}
                        for ch in range(2):
                            pe_bank(ch)
                            if ch == 0:
                                W("tensor", cg[0], cg[1])
                                if first_gate:
                                    W("tensor", "dv", dvy)

                            def fg(t, slg=slg, ch=ch):
                                for dc in range(NC):
                                    mm = t.matmul(bk(ch), WR[:, slg * 2048 + dc * 128:slg * 2048 + dc * 128 + 128],
                                                  YT[:, dc * T + ch * 512:dc * T + ch * 512 + 512],
                                                  start=(dc == 0), stop=(dc == NC - 1))
                                return mm
                            pcg[ch] = EMI("tensor", "pe", fg)
                        wr_consumed(slg)
                        if first_gate:
                            first_gate = False
                            accb = emit_cb()
                        slu, cu = wr_load(mk_wquarter(i_wu, ex, fc))
                        pcu = {}
                        for ch in range(2):
                            pe_bank(2 + ch)
                            if ch == 0:
                                W("tensor", cu[0], cu[1])

                            def fu(t, slu=slu, ch=ch):
                                for dc in range(NC):
                                    mm = t.matmul(bk(2 + ch), WR[:, slu * 2048 + dc * 128:slu * 2048 + dc * 128 + 128],
                                                  YT[:, dc * T + ch * 512:dc * T + ch * 512 + 512],
                                                  start=(dc == 0), stop=(dc == NC - 1))
                                return mm
                            pcu[ch] = EMI("tensor", "pe", fu)
                        wr_consumed(slu)
                        for ch in range(2):
                            W("scalar", "pe", pcg[ch])
                            W("scalar", "dv", prev_ascr_dv)
                            acsl = EMI("scalar", "ac", lambda s, ch=ch: s.activation(
                                silscr[:, :], bk(ch), AF.Silu))
                            bank_guard[ch] = ("ac", acsl)
                            W("vector", "ac", acsl)
                            W("vector", "ac", accb)
                            if first_write:
                                W("vector", "pe", a_guard_pe)
                                first_write = False
                            W("vector", "pe", pcu[ch])
                            dv1 = EMI("vector", "dv", lambda v, k=k, fc=fc, ch=ch:
                                      v.tensor_tensor(a_sl(k, fc, ch), silscr[:, :], bk(2 + ch),
                                                      op=ALU.mult))
                            bank_guard[2 + ch] = ("dv", dv1)
                            prev_ascr_dv = dv1
                            dv2 = EMI("vector", "dv", lambda v, k=k, fc=fc, ch=ch, csl=csl:
                                      v.tensor_tensor(a_sl(k, fc, ch), a_sl(k, fc, ch),
                                                      cb_slot[csl][:, ch * 512:ch * 512 + 512],
                                                      op=ALU.mult))
                    cbg[csl] = dv2
                a_ready = CNT["dv"]
                for dc in range(NC):
                    sld, cd = wd_load(mk_wdslot(grp, dc))
                    for ch in range(2):
                        b = 4 + (dc * 2 + ch) % 2
                        pe_bank(b)
                        W("tensor", cd[0], cd[1])
                        W("tensor", "dv", a_ready)

                        def fdn(t, sld=sld, b=b, G=G, ch=ch):
                            i = 0
                            for k in range(G):
                                for fc in range(FC):
                                    o = (k * FC + fc) * 128
                                    mm = t.matmul(bk(b), wd_slot[sld][:, o:o + 128],
                                                  a_sl(k, fc, ch),
                                                  start=(i == 0), stop=(i == G * FC - 1))
                                    i += 1
                            return mm
                        pdn = EMI("tensor", "pe", fdn)
                        W("vector", "pe", pdn)
                        dve = EMI("vector", "dv", lambda v, b=b, dc=dc, ch=ch: v.tensor_tensor(
                            XT[:, dc * T + ch * 512:dc * T + ch * 512 + 512], bk(b),
                            XT[:, dc * T + ch * 512:dc * T + ch * 512 + 512], op=ALU.add))
                        bank_guard[b] = ("dv", dve)
                        dve_dc[dc] = dve
                    wd_consumed(sld)
                a_guard_pe = CNT["pe"]
            return dve_dc

        def attn_phase():
            dvy2 = ln_phase()
            W("vector", "pe", CNT["pe"])
            for b in range(2, 8):
                pe_bank(b)

            def fz(v):
                for b in range(2, 8):
                    v.memset(bk(b), 0.0)
                v.memset(BD16[:, :], 0.0)
                return v.memset(expS0, 0.0)
            dvz = EMI("vector", "dv", fz)
            for b in range(2, 8):
                bank_guard[b] = ("dv", dvz)
            scatter_last = 0
            expS_guard = 0
            PT_guard = 0
            prev_exp_ac = 0
            prev_fao2 = 0
            prev_fao2h = [0, 0]
            dvbd = 0
            vh = {}
            send_cnt = [0, 0]

            def emit_readback(q):
                W("sync", "cc", q + 1)
                W("sync", "small", WS_SMALL)
                if q == 0:
                    EM("sync", lambda sy: vh.__setitem__(
                        "v", sy.value_load(hoffs[0:1, 0:1])))
                for r in range(2):
                    def frb(sy, r=r, q=q):
                        s2 = cc_out_q[q][:, :].rearrange("(r2 j p) c -> r2 p j c",
                                                         r2=2, p=128)[r, :, :, :]
                        return sy.dma_start(
                            out=STv.rearrange("p (j q2 r2 m) -> p j q2 r2 m",
                                              q2=2, r2=2, m=256)[:, :, q, r, :],
                            in_=s2[:, :, bass.ds(vh["v"], 256)],
                        ).then_inc(sems["rb"], 16)
                    EM("sync", frb)
                    inc("rb", 16)

            def emit_collective(q):
                W("gpsimd", "send", send_cnt[q])
                EM("gpsimd", lambda g, q=q: g.collective_compute(
                    "AllGather", ALU.bypass, replica_groups=RG,
                    ins=[cc_in_q[q][:, :].opt()],
                    outs=[cc_out_q[q][:, :].opt()]).then_inc(sems["cc"], 1))
                inc("cc")
            for h2 in range(2):
                pe_h_inner_prev = CNT["pe"]
                if h2 == 1:
                    emit_readback(0)
                for mg in range(MG):
                    if h2 == 1 and mg == 3:
                        emit_collective(0)
                    slq, cq = wr_load(mk_wcol(i_wqkvT, mg))
                    qb = mg % 3
                    pe_bank(qb)
                    W("tensor", cq[0], cq[1])
                    W("tensor", "dv", dvy2)

                    def fqk(t, slq=slq, qb=qb, h2=h2):
                        for dc in range(NC):
                            mm = t.matmul(bk(qb), WR[:, slq * 2048 + dc * 128:slq * 2048 + dc * 128 + 128],
                                          YT[:, dc * T + h2 * 512:dc * T + h2 * 512 + 512],
                                          start=(dc == 0), stop=(dc == NC - 1))
                        return mm
                    pcq = EMI("tensor", "pe", fqk)
                    wr_consumed(slq)
                    W("scalar", "pe", pcq)
                    acq = EMI("scalar", "ac", lambda s, mg=mg, qb=qb: s.activation(
                        qkvT[:, mg * 512:(mg + 1) * 512], bk(qb), AF.Identity,
                        bias=bq_sb[:, mg:mg + 1], scale=1.0))
                    bank_guard[qb] = ("ac", acq)
                ac_qkv = CNT["ac"]
                for b in range(3):
                    s_, c_ = bank_guard[b]
                    W("vector", s_, c_)
                W("vector", "ac", ac_qkv)
                dvz2 = EMI("vector", "dv", lambda v: (v.memset(bk(0), 0.0), v.memset(bk(1), 0.0),
                                                      v.memset(bk(2), 0.0))[-1])
                for b in range(3):
                    bank_guard[b] = ("dv", dvz2)
                for pr in range(2):
                    tiles = (2 * pr, 2 * pr + 1)
                    sc_pe = {}
                    for ti in tiles:
                        sb = 3 + (ti % 2)
                        pe_bank(sb)
                        W("tensor", "ac", ac_qkv)

                        def fsc(t, sb=sb, ti=ti):
                            for tok in range(128):
                                u, yy = tok // 32, tok % 32
                                tau = ti * 128 + tok
                                mm = t.matmul(
                                    PS[32 * u:32 * u + 16, sb * 512 + yy * 16:sb * 512 + yy * 16 + 16],
                                    qkv3[:, 0:16, tau], qkv3[:, 16:32, tau],
                                    start=True, stop=True, tile_position=(0, 32 * u))
                            return mm
                        sc_pe[ti] = EMI("tensor", "pe", fsc)
                    for ti in tiles:
                        par = ti % 2
                        sb = 3 + par
                        W("vector", "pe", sc_pe[ti])
                        W("vector", "ac", prev_exp_ac)

                        def fsm0(v, sb=sb, par=par):
                            sc3 = bk(sb).rearrange("p (y g) -> p y g", g=16)
                            return v.reduce_max(Msc[:, par * 32:par * 32 + 32], sc3, axis=AX.X)
                        c_ = EMI("vector", "dv", fsm0)
                        W("vector", "dv", c_)

                        def fsm(v, sb=sb, par=par):
                            sc3 = bk(sb).rearrange("p (y g) -> p y g", g=16)
                            ms = Msc[:, par * 32:par * 32 + 32]
                            return v.tensor_tensor(
                                scsub.rearrange("p (y g) -> p y g", g=16), sc3,
                                ms.rearrange("p (y o) -> p y o", o=1).broadcast_to([128, 32, 16]),
                                op=ALU.subtract)
                        dvsm = EMI("vector", "dv", fsm)
                        bank_guard[sb] = ("dv", dvsm)
                        W("scalar", "dv", dvsm)
                        W("scalar", "dv", expS_guard)
                        W("scalar", "pe", PT_guard)

                        def fex(s):
                            return s.activation(
                                expS0.rearrange("p (y g) -> p y g", g=32)[:, :, 0:16],
                                scsub.rearrange("p (y g) -> p y g", g=16), AF.Exp, scale=ISQ)
                        ace = EMI("scalar", "ac", fex)
                        prev_exp_ac = ace
                        W("vector", "ac", ace)

                        def fpt0(v, par=par):
                            e3 = expS0.rearrange("p (y g) -> p y g", g=32)[:, :, 0:16]
                            return v.reduce_sum(Msc[:, par * 32:par * 32 + 32], e3, axis=AX.X)
                        c_ = EMI("vector", "dv", fpt0)
                        W("vector", "dv", c_)

                        def fpt1(v, par=par):
                            return v.reciprocal(Rsb[:, par * 32:par * 32 + 32],
                                                Msc[:, par * 32:par * 32 + 32])
                        c_ = EMI("vector", "dv", fpt1)
                        W("vector", "dv", c_)

                        def fscale(v, par=par):
                            e3 = expS0.rearrange("p (y g) -> p y g", g=32)[:, :, 0:16]
                            rr = Rsb[:, par * 32:par * 32 + 32].rearrange(
                                "p (y o) -> p y o", o=1).broadcast_to([128, 32, 16])
                            return v.tensor_tensor(e3, e3, rr, op=ALU.mult)
                        c_ = EMI("vector", "dv", fscale)
                        W("vector", "dv", c_)

                        def fpt(v, par=par):
                            return v.transpose(PTv0, expS0)
                        ptdv = EMI("vector", "dv", fpt)
                        expS_guard = ptdv
                        W("tensor", "dv", ptdv)
                        for bi in range(4):
                            gy, jh = bi // 2, bi % 2
                            bp = bi % 2
                            vb = [5, 0][bp]
                            ob = [6, 1][bp]
                            ys = [gy * 16 + jh * 8 + k for k in range(8)]
                            if jh == 0:
                                W("vector", "pe", prev_fao2)

                                def fbd(v, gy=gy):
                                    for u in range(4):
                                        src = PTv0[32 * u:32 * u + 16, :].rearrange(
                                            "p (y h) -> p y h", h=32)[:, gy * 16:(gy + 1) * 16, 0:16]
                                        dst = BD16[32 * u:32 * u + 16, :].rearrange(
                                            "p (y c) -> p y c", c=64)[:, :, u * 16:u * 16 + 16]
                                        mm = v.tensor_copy(dst, src)
                                    return mm
                                dvbd = EMI("vector", "dv", fbd)
                            pe_bank(vb)

                            def fvtb(t, vb=vb, ti=ti, ys=ys):
                                for k, y in enumerate(ys):
                                    for u in range(4):
                                        tau = ti * 128 + u * 32 + y
                                        mm = t.transpose(
                                            bkb16(vb)[32 * u:32 * u + 16, k * 128:(k + 1) * 128],
                                            qkv3[:, 32:48, tau], identB[:, :],
                                            tile_position=(0, 32 * u))
                                return mm
                            pv = EMI("tensor", "pe", fvtb)
                            W("vector", "pe", pv)
                            W("vector", "pe", prev_fao2)
                            dv1 = EMI("vector", "dv", lambda v, vb=vb: v.tensor_copy(
                                v4b[:, :], bkb16(vb)[:, 0:1024]))
                            bank_guard[vb] = ("dv", dv1)
                            pe_bank(ob)
                            W("tensor", "dv", dv1)
                            W("tensor", "dv", dvbd)

                            def fao2(t, ob=ob, ys=ys):
                                for k, y in enumerate(ys):
                                    yl = y % 16
                                    mm = t.matmul(bk(ob)[:, k * 64:(k + 1) * 64],
                                                  v4b[:, k * 128:(k + 1) * 128],
                                                  BD16[:, yl * 64:(yl + 1) * 64],
                                                  start=True, stop=True)
                                return mm
                            pf = EMI("tensor", "pe", fao2)
                            prev_fao2 = pf
                            PT_guard = pf
                            W("vector", "pe", pf)

                            def fsctb(v, ob=ob, ti=ti, gy=gy, jh=jh, h2=h2):
                                src4 = bk(ob).rearrange("p (ys2 u h) -> p ys2 u h",
                                                        ys2=8, u=4, h=16)
                                dstv = sendv.rearrange("p (j c) -> p j c",
                                                       c=1024)[:, jh * 8:jh * 8 + 8, :]
                                dstl = dstv.rearrange("p j (l g) -> p j l g", g=32)
                                for u in range(4):
                                    mm = v.tensor_copy(
                                        dstl[:, :, h2 * 16:(h2 + 1) * 16, ti * 8 + u * 2 + gy],
                                        src4[:, :, u, :])
                                return mm
                            dv3 = EMI("vector", "dv", fsctb)
                            bank_guard[ob] = ("dv", dv3)
                            scatter_last = dv3
                W("sync", "dv", scatter_last)

                def fsend(sy, q=h2):
                    return sy.dma_start(
                        out=cc_in_q[q][:, :].rearrange("(j p) c -> p j c", p=128),
                        in_=sendv.rearrange("p (j q2 c) -> p j q2 c",
                                            q2=2, c=512)[:, :, q, :],
                    ).then_inc(sems["send"], 16)
                EM("sync", fsend)
                send_cnt[h2] = inc("send", 16)
            emit_readback(1)
            for q in range(2):
                W("tensor", "rb", 32 * (q + 1))
                for mg in range(NC):
                    if q == 0 and mg == 3:
                        emit_collective(1)
                    slw, cw = wr_load(mk_wcol(i_woutT, mg))
                    b = 2 * q + (mg % 2)
                    pe_bank(b)
                    W("tensor", cw[0], cw[1])

                    def fpj(t, slw=slw, b=b, mg=mg, q=q):
                        for j in range(NC):
                            mm = t.matmul(bk(b), WR[:, slw * 2048 + j * 128:slw * 2048 + j * 128 + 128],
                                          STv[:, j * 1024 + q * 512:j * 1024 + q * 512 + 512],
                                          start=(j == 0), stop=(j == NC - 1))
                        return mm
                    ppj = EMI("tensor", "pe", fpj)
                    W("vector", "pe", ppj)

                    def fpe(v, b=b, mg=mg, q=q):
                        xr = XT[:, mg * T:(mg + 1) * T].rearrange(
                            "p (hp r2 q2 g) -> p hp r2 q2 g", hp=8, r2=2, q2=2, g=32)
                        for r in range(2):
                            xv = xr[:, :, r, q, :]
                            bkv = bk(b)[:, r * 256:(r + 1) * 256].rearrange(
                                "p (hp g) -> p hp g", g=32)
                            mm = v.scalar_tensor_tensor(
                                xv, bkv, bo_sb[:, mg:mg + 1], xv,
                                op0=ALU.add, op1=ALU.add)
                        return mm
                    dvp = EMI("vector", "dv", fpe)
                    bank_guard[b] = ("dv", dvp)
                    wr_consumed(slw)
            return CNT["dv"]

        if debug:
            W("sync", "dv", CNT["dv"])
            EM("sync", lambda sy: sy.dma_start(out=dbg["d_x0"], in_=XT[:, :]).then_inc(sems["send"], 16))
            inc("send", 16)
        dv_xt0 = CNT["dv"]
        dvy1_ = ln_phase()
        dvc = router_phase(dv_xt0)
        if debug:
            W("sync", "dv", max(dvy1_, dvc))
            EM("sync", lambda sy: sy.dma_start(out=dbg["d_y1"], in_=YT[:, :]).then_inc(sems["send"], 16))
            EM("sync", lambda sy: sy.dma_start(out=dbg["d_comb"], in_=combT[0:32, :]).then_inc(sems["send"], 16))
            EM("sync", lambda sy: sy.dma_start(out=dbg["d_rt"][0:18, :], in_=rt_sb[0:18, :]).then_inc(sems["send"], 16))
            EM("sync", lambda sy: sy.dma_start(out=dbg["d_l17"], in_=l17a[:, 0:256]).then_inc(sems["send"], 16))
            EM("sync", lambda sy: sy.dma_start(out=dbg["d_m8"], in_=m8a[:, 0:64]).then_inc(sems["send"], 16))
            EM("sync", lambda sy: sy.dma_start(out=dbg["d_mask"], in_=maska[:, 0:128]).then_inc(sems["send"], 16))
            EM("sync", lambda sy: sy.dma_start(out=dbg["d_me"], in_=expva[:, 0:128]).then_inc(sems["send"], 16))
            EM("sync", lambda sy: sy.dma_start(out=dbg["d_c17"], in_=c17a[:, 0:256]).then_inc(sems["send"], 16))
            EM("sync", lambda sy: sy.dma_start(out=dbg["d_nmv"], in_=nmsnapv[:, :]).then_inc(sems["send"], 16))
            EM("sync", lambda sy: sy.dma_start(out=dbg["d_nma"], in_=nmsnapa[:, :]).then_inc(sems["send"], 16))
            inc("send", 160)
        dv1 = moe_phase(dvc, 0, dvy1_)
        if debug:
            W("sync", "dv", max(dv1.values()))
            EM("sync", lambda sy: sy.dma_start(out=dbg["d_x1"], in_=XT[:, :]).then_inc(sems["send"], 16))
            inc("send", 16)
        dv2 = attn_phase()
        if debug:
            W("sync", "dv", dv2)
            EM("sync", lambda sy: sy.dma_start(out=dbg["d_x2"], in_=XT[:, :]).then_inc(sems["send"], 16))
            inc("send", 16)
        dv_xt1 = CNT["dv"]
        dvy3_ = ln_phase()
        dvc3 = router_phase(dv_xt1)
        wd_state["guard"] = [CNT["pe"], CNT["pe"]]
        dv3 = moe_phase(dvc3, 0, dvy3_)

        out_guard = [0, 0]
        for tt in range(8):
            sl = tt % 2
            W("vector", "xs%d" % sl, out_guard[sl])
            dvo = 0
            for dc in range(NC):
                pb = dc % 2
                pe_bank(pb)
                W("tensor", "dv", dv3[dc])
                pco = EMI("tensor", "pe", lambda t, tt=tt, dc=dc, pb=pb: t.transpose(
                    bk(pb, 0, 128).bitcast(F32)[:, 0:128],
                    XT[:, dc * T + tt * 128:dc * T + (tt + 1) * 128], identF[:, :]))
                W("vector", "pe", pco)
                dvo = EMI("vector", "dv", lambda v, dc=dc, pb=pb, sl=sl: v.tensor_copy(
                    xstage[sl][:, dc * 128:(dc + 1) * 128],
                    bk(pb, 0, 128).bitcast(F32)[:, 0:128]))
                bank_guard[pb] = ("dv", dvo)
            W("sync", "dv", dvo)
            EM("sync", lambda sy, tt=tt, sl=sl: sy.dma_start(
                out=o_y[tt * 128:(tt + 1) * 128, :], in_=xstage[sl][:, :]).then_inc(sems["xs%d" % sl], 16))
            out_guard[sl] = inc("xs%d" % sl, 16)
        W("sync", "xs0", CNT["xs0"])
        W("sync", "xs1", CNT["xs1"])
        W("sync", "send", CNT["send"])

        def runner(name):
            def run(engine):
                for item in Q[name]:
                    if isinstance(item, tuple) and item[0] == "W":
                        engine.wait_ge(sems[item[1]], item[2])
                    else:
                        item(engine)
            return run

        @blk.tensor
        def _(t):
            runner("tensor")(t)

        @blk.vector
        def _(v):
            runner("vector")(v)

        @blk.scalar
        def _(s):
            runner("scalar")(s)

        @blk.gpsimd
        def _(g):
            runner("gpsimd")(g)

        @blk.sync
        def _(sy):
            runner("sync")(sy)

    return nc


def _prep(inputs):
    g = lambda k: np.ascontiguousarray(np.asarray(inputs[k], dtype=np.float32))
    x = g("L0_attn_x")
    wqkvT = np.ascontiguousarray(
        g("L0_attn_w_qkv").reshape(48, 128, 16, 128).transpose(0, 3, 2, 1).reshape(48, 128, 2048))
    woutT = np.ascontiguousarray(
        g("L0_attn_w_out").reshape(16, 128, 16, 128).transpose(0, 3, 2, 1).reshape(16, 128, 2048))
    g17 = np.concatenate([g("L0_ffn_gate_w").T, g("L0_ffn_shared_gate_w"),
                          np.zeros((2048, 1), np.float32)], axis=1)
    g17 = np.ascontiguousarray(g17.reshape(16, 128, 18).transpose(1, 0, 2))
    bq = np.ascontiguousarray(g("L0_attn_b_qkv").reshape(48, 128).T)
    bo = np.ascontiguousarray(g("L0_attn_b_out").reshape(16, 128).T)
    wg0 = np.concatenate([g("L0_ffn_Wg"), g("L0_ffn_sWg")[None]], axis=0)
    wu0 = np.concatenate([g("L0_ffn_Wu"), g("L0_ffn_sWu")[None]], axis=0)
    wd0 = np.concatenate([g("L0_ffn_Wd"), g("L0_ffn_sWd")[None]], axis=0)
    wg = np.ascontiguousarray(
        wg0.reshape(17, 16, 128, 4, 128).transpose(0, 3, 2, 1, 4).reshape(17, 4, 128, 2048))
    wu = np.ascontiguousarray(
        wu0.reshape(17, 16, 128, 4, 128).transpose(0, 3, 2, 1, 4).reshape(17, 4, 128, 2048))
    wd = np.ascontiguousarray(
        wd0.reshape(17, 4, 128, 16, 128).transpose(3, 2, 0, 1, 4).reshape(16, 128, 17 * 512))
    return x, wqkvT, woutT, g17, bq, bo, wg, wu, wd


def kernel(**inputs):
    x, wqkvT, woutT, g17, bq, bo, wg, wu, wd = _prep(inputs)
    nc = build_nc(False)
    in_maps = []
    for c in range(8):
        b, hf = c // 2, c % 2
        in_maps.append({
            "i_x": np.ascontiguousarray(x[b, hf * 1024:(hf + 1) * 1024, :]),
            "i_wqkvT": wqkvT, "i_woutT": woutT, "i_g17": g17,
            "i_bq": bq, "i_bo": bo, "i_wg": wg, "i_wu": wu, "i_wd": wd,
            "i_hoff": np.array([[hf * 256]], dtype=np.int32),
        })
    res = run_bass_kernel_spmd(nc, in_maps, core_ids=list(range(8)))
    out = np.empty((4, 2048, 2048), np.float32)
    for c in range(8):
        out[c // 2, (c % 2) * 1024:(c % 2 + 1) * 1024, :] = np.asarray(res.results[c]["o_y"])
    return out


# version 19 (final)
# speedup vs baseline: 1.0933x; 1.0933x over previous
import numpy as np
from contextlib import ExitStack

import concourse.bass as bass
import concourse.mybir as mybir
from concourse.bass_utils import run_bass_kernel_spmd

F32 = mybir.dt.float32
BF16 = mybir.dt.bfloat16
I32 = mybir.dt.int32
AF = mybir.ActivationFunctionType
ALU = mybir.AluOpType
AX = mybir.AxisListType

D = 2048
NC = 16
T = 1024
E = 17
FC = 4
QKV = 6144
MG = 48
ISQ = float(1.0 / np.sqrt(2048.0))
EPS = 1e-5
GROUPS = [list(range(0, 6)), list(range(6, 12)), list(range(12, 17))]
RG = [[0, 1], [2, 3], [4, 5], [6, 7]]


def build_nc(debug=False):
    nc = bass.Bass("TRN2", target_bir_lowering=False, detect_race_conditions=False)

    i_x = nc.dram_tensor("i_x", [D, T], F32, kind="ExternalInput").ap()
    i_wqkvT = nc.dram_tensor("i_wqkvT", [MG, 128, D], F32, kind="ExternalInput").ap()
    i_woutT = nc.dram_tensor("i_woutT", [NC, 128, D], F32, kind="ExternalInput").ap()
    i_g17 = nc.dram_tensor("i_g17", [128, NC, 18], F32, kind="ExternalInput").ap()
    i_bq = nc.dram_tensor("i_bq", [128, MG], F32, kind="ExternalInput").ap()
    i_bo = nc.dram_tensor("i_bo", [128, NC], F32, kind="ExternalInput").ap()
    i_wg = nc.dram_tensor("i_wg", [E, FC, 128, D], F32, kind="ExternalInput").ap()
    i_wu = nc.dram_tensor("i_wu", [E, FC, 128, D], F32, kind="ExternalInput").ap()
    i_wd = nc.dram_tensor("i_wd", [NC, 128, E * 512], F32, kind="ExternalInput").ap()
    i_hoff = nc.dram_tensor("i_hoff", [1, 1], I32, kind="ExternalInput").ap()
    o_y = nc.dram_tensor("o_y", [D, T], F32, kind="ExternalOutput").ap()
    dbg = {}
    if debug:
        for nm in ("d_x1", "d_x2"):
            dbg[nm] = nc.dram_tensor(nm, [128, NC * T], F32, kind="ExternalOutput").ap()
        dbg["d_x0"] = nc.dram_tensor("d_x0", [128, NC * T], F32, kind="ExternalOutput").ap()
        dbg["d_y1"] = nc.dram_tensor("d_y1", [128, NC * T], BF16, kind="ExternalOutput").ap()
        dbg["d_comb"] = nc.dram_tensor("d_comb", [32, T], BF16, kind="ExternalOutput").ap()
        dbg["d_rt"] = nc.dram_tensor("d_rt", [32, T], F32, kind="ExternalOutput").ap()
        dbg["d_l17"] = nc.dram_tensor("d_l17", [128, 256], F32, kind="ExternalOutput").ap()
        dbg["d_m8"] = nc.dram_tensor("d_m8", [128, 64], F32, kind="ExternalOutput").ap()
        dbg["d_mask"] = nc.dram_tensor("d_mask", [128, 128], F32, kind="ExternalOutput").ap()
        dbg["d_me"] = nc.dram_tensor("d_me", [128, 128], F32, kind="ExternalOutput").ap()
        dbg["d_c17"] = nc.dram_tensor("d_c17", [128, 256], F32, kind="ExternalOutput").ap()
        dbg["d_nmv"] = nc.dram_tensor("d_nmv", [128, 8], F32, kind="ExternalOutput").ap()
        dbg["d_nma"] = nc.dram_tensor("d_nma", [128, 8], F32, kind="ExternalOutput").ap()

    cc_in_q = [nc.dram_tensor("cc_in0", [2048, 512], BF16),
               nc.dram_tensor("cc_in1", [2048, 512], BF16)]
    cc_out_q = [nc.dram_tensor("cc_out0", [4096, 512], BF16),
                nc.dram_tensor("cc_out1", [4096, 512], BF16)]

    ENGS = ("tensor", "vector", "scalar", "gpsimd", "sync")
    Q = {en: [] for en in ENGS}
    CNT = {s: 0 for s in ("pe", "dv", "ac", "pl", "cc", "small", "send", "rb", "pc", "xs0", "xs1", "wr0", "wr1", "wr2", "wd0", "wd1")}
    LASTW = {en: {} for en in ENGS}

    def inc(s, n=1):
        CNT[s] += n
        return CNT[s]

    def W(eng, s, val):
        if val <= 0:
            return
        if LASTW[eng].get(s, 0) >= val:
            return
        LASTW[eng][s] = val
        Q[eng].append(("W", s, val))

    def EM(eng, fn):
        Q[eng].append(fn)

    with ExitStack() as st:
        en_ = st.enter_context
        XT = en_(nc.sbuf_tensor("XT", [128, NC * T], F32))
        YT = en_(nc.sbuf_tensor("YT", [128, NC * T], BF16))
        AR1 = en_(nc.sbuf_tensor("AR1", [128, 24576], BF16))
        AR2 = en_(nc.sbuf_tensor("AR2", [128, 16384], BF16))
        WR = en_(nc.sbuf_tensor("WR", [128, 3 * 2048], BF16))
        MISC = en_(nc.sbuf_tensor("MISC", [128, 4096], BF16))
        g17s = en_(nc.sbuf_tensor("g17s", [128, NC * 18], F32))
        bq_sb = en_(nc.sbuf_tensor("bq_sb", [128, MG], F32))
        bo_sb = en_(nc.sbuf_tensor("bo_sb", [128, NC], F32))
        identB = en_(nc.sbuf_tensor("identB", [128, 128], BF16))
        identF = en_(nc.sbuf_tensor("identF", [128, 128], F32))
        ones1f = en_(nc.sbuf_tensor("ones1f", [1, 128], F32))
        onescolB = en_(nc.sbuf_tensor("onescolB", [128, 1], BF16))
        ones128B = en_(nc.sbuf_tensor("ones128B", [128, 128], BF16))
        onescolF = en_(nc.sbuf_tensor("onescolF", [128, 1], F32))
        u17 = en_(nc.sbuf_tensor("u17", [1, 18], F32))
        Srow = en_(nc.sbuf_tensor("Srow", [1, 18], F32))
        silscr = en_(nc.sbuf_tensor("silscr", [128, 512], F32))
        v4sb = en_(nc.sbuf_tensor("v4sb", [128, 2 * 128], BF16))
        sbao = en_(nc.sbuf_tensor("sbao", [128, 2 * 128], BF16))
        Rsb = en_(nc.sbuf_tensor("Rsb", [128, 2 * 32], F32))
        Msc = en_(nc.sbuf_tensor("Msc", [128, 2 * 32], F32))
        hoffs = en_(nc.sbuf_tensor("hoffs", [1, 1], I32))
        nmsnapv = en_(nc.sbuf_tensor("nmsnapv", [128, 8], F32))
        nmsnapa = en_(nc.sbuf_tensor("nmsnapa", [128, 8], F32))
        SEL = en_(nc.sbuf_tensor("SEL", [32, E * 128], BF16))
        PS = en_(nc.psum_tensor("PS", [128, 4096], F32))
        sems = {s: en_(nc.semaphore("s_" + s)) for s in CNT}
        blk = en_(nc.Block())

        def EMI(eng, sn, fn):
            def wrapped(engine, fn=fn, sn=sn):
                r = fn(engine)
                r.then_inc(sems[sn], 1)
            Q[eng].append(wrapped)
            return inc(sn)

        def bk(b, off=0, w=512):
            return PS[:, b * 512 + off:b * 512 + off + w]

        def bkp(b, p0, pn, off=0, w=512):
            return PS[p0:p0 + pn, b * 512 + off:b * 512 + off + w]

        def bkb16(b):
            return PS[:, b * 512:(b + 1) * 512].bitcast(BF16)

        bank_guard = {b: ("dv", 0) for b in range(8)}

        def pe_bank(b):
            s, c = bank_guard[b]
            W("tensor", s, c)

        def hard_sync():
            for eng in ("tensor", "vector", "scalar"):
                for s in ("pe", "dv", "ac"):
                    W(eng, s, CNT[s])

        def ar1b(o, n):
            return AR1[:, o:o + n]

        def ar1f(o, n):
            return AR1[:, o:o + 2 * n].bitcast(F32)

        def ar2b(o, n):
            return AR2[:, o:o + n]

        def ar2f(o, n):
            return AR2[:, o:o + 2 * n].bitcast(F32)

        qkvT = ar1b(0, MG * 512)
        qkv3 = qkvT.rearrange("p (mg t) -> p mg t", t=512)

        def a_sl(k, fc, ch):
            o = (k * FC + fc) * T + ch * 512
            return ar1b(o, 512)

        xstage = [ar1f(0, 2048), ar1f(4096, 2048)]
        wd_slot = [ar2b(0, 3072), ar2b(3072, 3072)]
        cb_slot = [ar2b(6144, 1024), ar2b(7168, 1024)]
        rt_sb = AR2[0:32, 8192:10240].bitcast(F32)
        combT = AR2[0:32, 10240:11264]
        c17a = ar2f(11264, 256)
        l17a = ar2f(11776, 256)
        expva = ar2f(12288, 128)
        maska = ar2f(12544, 128)
        m8a = ar2f(12800, 64)
        nm1a = ar2f(12928, 8)
        siga = ar2f(12944, 8)
        varrow = AR2[0:1, 13312:15360].bitcast(F32)
        xb16s = [ar2b(0, 1024), ar2b(2048, 1024)]
        sqbs = [ar2b(1024, 1024), ar2b(3072, 1024)]
        sendv = AR2
        STv = AR2

        v4b = silscr[:, 0:512].bitcast(BF16)
        negm_r = MISC[0:1, 0:2048].bitcast(F32)
        rstd_r = MISC[0:1, 2048:4096].bitcast(F32)
        expS0 = MISC[:, 0:1024]
        PTv0 = MISC[:, 1024:2048]
        scsub = MISC[:, 2048:3072].bitcast(F32)
        BD16 = MISC[:, 3072:4096]

        wr_state = {"i": 0, "guard": [0, 0, 0]}

        def wr_load(mk):
            sl = wr_state["i"] % 3
            sn = "wr%d" % sl
            W("gpsimd", "pe", wr_state["guard"][sl])
            slot_ap = WR[:, sl * 2048:(sl + 1) * 2048]
            EM("gpsimd", lambda g, mk=mk, a=slot_ap, sn=sn: mk(g, a).then_inc(sems[sn], 16))
            c = inc(sn, 16)
            wr_state["i"] += 1
            return sl, (sn, c)

        def wr_consumed(sl):
            wr_state["guard"][sl] = CNT["pe"]

        wd_state = {"i": 0, "guard": [0, 0]}

        def wd_load(mk):
            sl = wd_state["i"] % 2
            sn = "wd%d" % sl
            W("gpsimd", "dv", DBG_DV0)
            W("gpsimd", "pe", wd_state["guard"][sl])
            EM("gpsimd", lambda g, mk=mk, a=wd_slot[sl], sn=sn: mk(g, a).then_inc(sems[sn], 16))
            c = inc(sn, 16)
            wd_state["i"] += 1
            return sl, (sn, c)

        def wd_consumed(sl):
            wd_state["guard"][sl] = CNT["pe"]

        def mk_wquarter(base, ex, fc):
            def mk(g, slot_ap, base=base, ex=ex, fc=fc):
                return g.dma_start(out=slot_ap, in_=base[ex, fc, :, :])
            return mk

        def mk_wdslot(grp, dc):
            e0, e1 = grp[0], grp[-1] + 1
            def mk(g, slot_ap, e0=e0, e1=e1, dc=dc):
                ne = e1 - e0
                return g.dma_start(
                    out=slot_ap[:, 0:ne * FC * 128],
                    in_=i_wd[dc, :, e0 * 512:e1 * 512])
            return mk

        def mk_wcol(base, mg):
            def mk(g, slot_ap, base=base, mg=mg):
                return g.dma_start(out=slot_ap, in_=base[mg, :, :])
            return mk

        def f_init(g):
            g.memset(identB[:, :], 0.0)
            g.memset(identF[:, :], 0.0)
            g.memset(ones1f[:, :], 1.0)
            g.memset(onescolB[:, :], 1.0)
            g.memset(ones128B[:, :], 1.0)
            g.memset(onescolF[:, :], 1.0)
            g.memset(u17[:, :], 0.0)
            g.memset(SEL[:, :], 0.0).then_inc(sems["pl"], 1)
            g.wait_ge(sems["pl"], PL0)
            g.memset(u17[:, 17:18], 1.0)
            g.affine_select(out=identB[:, :], in_=identB[:, :], compare_op=ALU.not_equal,
                            fill=1.0, base=0, pattern=[[-1, 128]], channel_multiplier=1)
            g.affine_select(out=identF[:, :], in_=identF[:, :], compare_op=ALU.not_equal,
                            fill=1.0, base=0, pattern=[[-1, 128]], channel_multiplier=1)
            return g.affine_select(
                out=SEL.rearrange("p (e c) -> p e c", c=128),
                in_=SEL.rearrange("p (e c) -> p e c", c=128),
                compare_op=ALU.not_equal, fill=1.0, base=0,
                pattern=[[-1, E], [0, 128]], channel_multiplier=1)
        PL0 = CNT["pl"] + 1
        inc("pl")
        PL_INIT = EMI("gpsimd", "pl", f_init)

        def f_small(sy):
            sy.dma_start(out=g17s.rearrange("p (c e) -> p c e", e=18),
                         in_=i_g17).then_inc(sems["small"], 16)
            sy.dma_start(out=bq_sb[:, :], in_=i_bq).then_inc(sems["small"], 16)
            sy.dma_start(out=bo_sb[:, :], in_=i_bo).then_inc(sems["small"], 16)
            sy.dma_start(out=hoffs[:, :], in_=i_hoff).then_inc(sems["small"], 16)
        EM("sync", f_small)
        WS_SMALL = inc("small", 64)

        DBG_DV0 = EMI("vector", "dv", lambda v: v.memset(AR2[:, :], 0.0))

        W("tensor", "small", WS_SMALL)
        W("tensor", "pl", PL_INIT)

        def f_srow(t):
            for dc in range(NC):
                mm = t.matmul(bkp(7, 0, 1, 0, 18)[0:1, 0:18],
                              onescolF[:, :], g17s[:, dc * 18:(dc + 1) * 18],
                              start=(dc == 0), stop=(dc == NC - 1))
            return mm
        pc = EMI("tensor", "pe", f_srow)
        W("vector", "pe", pc)
        dvc = EMI("vector", "dv", lambda v: v.tensor_copy(Srow[0:1, 0:18],
                                                          bkp(7, 0, 1, 0, 18)[0:1, 0:18]))
        bank_guard[7] = ("dv", dvc)

        for half in range(2):
            def fx(sy, half=half):
                return sy.dma_start(
                    out=XT.rearrange("p (dc t) -> p dc t", t=T)[:, half * 8:(half + 1) * 8, :],
                    in_=i_x.rearrange("(dc p) t -> p dc t", p=128)[:, half * 8:(half + 1) * 8, :],
                ).then_inc(sems["xs0"], 16)
            EM("sync", fx)
        XS_IN = inc("xs0", 32)
        W("tensor", "pl", PL_INIT)
        for eng in ("tensor", "vector", "scalar"):
            W(eng, "xs0", XS_IN)

        def ln_phase():
            W("vector", "pe", CNT["pe"])
            W("scalar", "pe", CNT["pe"])
            W("vector", "dv", CNT["dv"])
            W("scalar", "dv", CNT["dv"])
            for b in range(4, 8):
                pe_bank(b)
            pe_fm = {}
            for dc in range(NC):
                par = dc % 2
                if dc >= 2:
                    W("vector", "pe", pe_fm[dc - 2])
                    W("scalar", "pe", pe_fm[dc - 2])
                dvx = EMI("vector", "dv", lambda v, dc=dc, par=par: v.tensor_copy(
                    xb16s[par][:, :], XT[:, dc * T:(dc + 1) * T]))
                acs = EMI("scalar", "ac", lambda s, dc=dc, par=par: s.activation(
                    sqbs[par][:, :], XT[:, dc * T:(dc + 1) * T], AF.Square))
                W("tensor", "dv", dvx)
                W("tensor", "ac", acs)

                def fm(t, dc=dc, par=par):
                    for ch in range(2):
                        t.matmul(bk(4 + ch), ones128B[:, :],
                                 xb16s[par][:, ch * 512:(ch + 1) * 512],
                                 start=(dc == 0), stop=(dc == NC - 1))
                    for ch in range(2):
                        mm = t.matmul(bk(6 + ch), ones128B[:, :],
                                      sqbs[par][:, ch * 512:(ch + 1) * 512],
                                      start=(dc == 0), stop=(dc == NC - 1))
                    return mm
                pe_fm[dc] = EMI("tensor", "pe", fm)
            W("vector", "pe", pe_fm[NC - 1])

            def frow1(v):
                v.tensor_scalar(negm_r[0:1, 0:512], bkp(4, 0, 1), -1.0 / 2048.0,
                                None, op0=ALU.mult)
                v.tensor_scalar(negm_r[0:1, 512:1024], bkp(5, 0, 1), -1.0 / 2048.0,
                                None, op0=ALU.mult)
                v.tensor_scalar(varrow[0:1, 0:512], bkp(6, 0, 1), 1.0 / 2048.0,
                                EPS, op0=ALU.mult, op1=ALU.add)
                return v.tensor_scalar(varrow[0:1, 512:1024], bkp(7, 0, 1), 1.0 / 2048.0,
                                       EPS, op0=ALU.mult, op1=ALU.add)
            c_ = EMI("vector", "dv", frow1)
            for b in range(4, 8):
                bank_guard[b] = ("dv", c_)
            W("vector", "dv", c_)

            def frow2(v):
                v.tensor_tensor(rstd_r[0:1, :], negm_r[0:1, :], negm_r[0:1, :], op=ALU.mult)
                return v.tensor_tensor(varrow[0:1, :], varrow[0:1, :], rstd_r[0:1, :],
                                       op=ALU.subtract)
            dvrows = EMI("vector", "dv", frow2)
            W("scalar", "dv", dvrows)

            def frs(s):
                return s.activation(rstd_r[0:1, :], varrow[0:1, :], AF.Sqrt)
            acr = EMI("scalar", "ac", frs)
            W("vector", "ac", acr)
            dvrec = EMI("vector", "dv", lambda v: v.reciprocal(rstd_r[0:1, :],
                                                               rstd_r[0:1, :]))
            W("tensor", "dv", dvrec)
            for b in range(4, 8):
                pe_bank(b)

            def fbc(t):
                t.matmul(bk(4), ones1f[:, :], rstd_r[0:1, 0:512], start=True, stop=True)
                t.matmul(bk(5), ones1f[:, :], rstd_r[0:1, 512:1024], start=True, stop=True)
                t.matmul(bk(6), ones1f[:, :], negm_r[0:1, 0:512], start=True, stop=True)
                return t.matmul(bk(7), ones1f[:, :], negm_r[0:1, 512:1024], start=True, stop=True)
            pebc = EMI("tensor", "pe", fbc)
            W("vector", "pe", pebc)

            def fy(v):
                for dc in range(NC):
                    for ch in range(2):
                        sl = XT[:, dc * T + ch * 512:dc * T + ch * 512 + 512]
                        v.tensor_tensor(silscr[:, :], sl, bk(6 + ch), op=ALU.add)
                        mm = v.tensor_tensor(YT[:, dc * T + ch * 512:dc * T + ch * 512 + 512],
                                             silscr[:, :], bk(4 + ch), op=ALU.mult)
                return mm
            dvy = EMI("vector", "dv", fy)
            for b in range(4, 8):
                bank_guard[b] = ("dv", dvy)
            return dvy

        def router_phase(dv_xt=None):
            W("tensor", "dv", CNT["dv"] if dv_xt is None else dv_xt)
            pe_bank(0)
            pe_bank(1)

            def frt(t):
                for dc in range(NC):
                    for ch in range(2):
                        t.matmul(PS[0:18, ch * 512:(ch + 1) * 512],
                                 g17s[:, dc * 18:(dc + 1) * 18],
                                 XT[:, dc * T + ch * 512:dc * T + ch * 512 + 512],
                                 start=(dc == 0), stop=False)
                for ch in range(2):
                    t.matmul(PS[0:18, ch * 512:(ch + 1) * 512], u17[:, :],
                             rstd_r[0:1, ch * 512:(ch + 1) * 512], start=False, stop=False)
                t.matmul(PS[0:18, 0:512], Srow[0:1, 0:18],
                         negm_r[0:1, 0:512], start=False, stop=True)
                return t.matmul(PS[0:18, 512:1024], Srow[0:1, 0:18],
                                negm_r[0:1, 512:1024], start=False, stop=True)
            pert = EMI("tensor", "pe", frt)
            W("vector", "pe", pert)
            dvrt = EMI("vector", "dv", lambda v: v.tensor_copy(rt_sb[0:18, :], PS[0:18, 0:1024]))
            bank_guard[0] = ("dv", dvrt)
            bank_guard[1] = ("dv", dvrt)
            W("tensor", "dv", dvrt)
            pe_bank(2)
            pe_bank(3)

            def ftp(t):
                for tt in range(8):
                    b = 2 + tt // 4
                    off = (tt % 4) * 32
                    mm = t.transpose(bkp(b, 0, 128, off, 32)[:, 0:18],
                                     rt_sb[0:18, tt * 128:(tt + 1) * 128], identF[0:18, 0:18])
                return mm
            petp = EMI("tensor", "pe", ftp)
            W("vector", "pe", petp)

            def fl17a(v):
                for tt in range(8):
                    b = 2 + tt // 4
                    off = (tt % 4) * 32
                    tp = bkp(b, 0, 128, off, 32)
                    mm = v.tensor_scalar(l17a[:, tt * 32:tt * 32 + 17], tp[:, 0:17],
                                         tp[:, 17:18], None, op0=ALU.mult)
                return mm
            c_ = EMI("vector", "dv", fl17a)
            W("vector", "dv", c_)

            def fl17b(v):
                for tt in range(8):
                    mm = v.max(out=m8a[:, tt * 8:(tt + 1) * 8], in_=l17a[:, tt * 32:tt * 32 + 16])
                return mm
            c_ = EMI("vector", "dv", fl17b)
            W("vector", "dv", c_)

            def fl17(v):
                for tt in range(8):
                    v.tensor_scalar(nm1a[:, tt:tt + 1], m8a[:, tt * 8:tt * 8 + 1], -1.0,
                                    None, op0=ALU.mult)
                    v.tensor_scalar(maska[:, tt * 16:(tt + 1) * 16], l17a[:, tt * 32:tt * 32 + 16],
                                    m8a[:, tt * 8 + 3:tt * 8 + 4], None, op0=ALU.is_ge)
                return v.tensor_copy(siga[:, 0:8],
                                     l17a.rearrange("p (tt c) -> p tt c", c=32)[:, :, 16])
            dvl = EMI("vector", "dv", fl17)
            if debug and "nmsnap" not in dbg:
                dbg["nmsnap"] = True
                W("vector", "dv", dvl)
                EMI("vector", "dv", lambda v: v.tensor_copy(nmsnapv[:, :], nm1a[:, 0:8]))
                W("scalar", "dv", CNT["dv"])
                EMI("scalar", "ac", lambda s: s.copy(nmsnapa[:, :], nm1a[:, 0:8]))
                hard_sync()
            bank_guard[2] = ("dv", dvl)
            bank_guard[3] = ("dv", dvl)
            W("scalar", "dv", dvl)

            def fexp(s):
                for tt in range(8):
                    s.activation(expva[:, tt * 16:(tt + 1) * 16], l17a[:, tt * 32:tt * 32 + 16],
                                 AF.Exp, bias=nm1a[:, tt:tt + 1])
                s.drain()
                return s.activation(nm1a[:, 0:8], siga[:, 0:8], AF.Sigmoid)
            acx = EMI("scalar", "ac", fexp)
            W("vector", "ac", acx)

            def fcomb0(v):
                for tt in range(8):
                    mm = v.tensor_tensor(expva[:, tt * 16:(tt + 1) * 16],
                                         expva[:, tt * 16:(tt + 1) * 16],
                                         maska[:, tt * 16:(tt + 1) * 16], op=ALU.mult)
                return mm
            c_ = EMI("vector", "dv", fcomb0)
            W("vector", "dv", c_)

            def fcomb1(v):
                for tt in range(8):
                    mm = v.reduce_sum(m8a[:, tt * 8:tt * 8 + 1], expva[:, tt * 16:(tt + 1) * 16],
                                      axis=AX.X)
                return mm
            c_ = EMI("vector", "dv", fcomb1)
            W("vector", "dv", c_)

            def fcomb2(v):
                for tt in range(8):
                    mm = v.reciprocal(m8a[:, tt * 8 + 1:tt * 8 + 2], m8a[:, tt * 8:tt * 8 + 1])
                return mm
            c_ = EMI("vector", "dv", fcomb2)
            W("vector", "dv", c_)

            def fcomb(v):
                for tt in range(8):
                    v.tensor_scalar(c17a[:, tt * 32:tt * 32 + 16], expva[:, tt * 16:(tt + 1) * 16],
                                    m8a[:, tt * 8 + 1:tt * 8 + 2], 0.5, op0=ALU.mult, op1=ALU.mult)
                return v.tensor_scalar(c17a.rearrange("p (tt c) -> p tt c", c=32)[:, :, 16],
                                       nm1a[:, 0:8], 0.5, None, op0=ALU.mult)
            dvcb = EMI("vector", "dv", fcomb)
            W("tensor", "dv", dvcb)
            pe_bank(2)
            pe_bank(3)

            def fct(t):
                for tt in range(8):
                    b = 2 + tt // 4
                    off = (tt % 4) * 128
                    mm = t.transpose(bkp(b, 0, 32, off, 128)[0:17, :],
                                     c17a[:, tt * 32:tt * 32 + 17], identF[:, :])
                return mm
            pct = EMI("tensor", "pe", fct)
            W("vector", "pe", pct)

            def fce(v):
                v.memset(combT[0:32, :], 0.0)
                for tt in range(8):
                    b = 2 + tt // 4
                    off = (tt % 4) * 128
                    mm = v.tensor_copy(combT[0:17, tt * 128:(tt + 1) * 128],
                                       bkp(b, 0, 32, off, 128)[0:17, :])
                return mm
            dvct = EMI("vector", "dv", fce)
            bank_guard[2] = ("dv", dvct)
            bank_guard[3] = ("dv", dvct)
            return dvct

        def moe_phase(dv_combT, a_guard_pe0, dvy):
            dve_dc = {}
            cbg = [0, 0]
            a_guard_pe = a_guard_pe0
            prev_ascr_dv = 0
            first_gate = True
            for gi, grp in enumerate(GROUPS):
                G = len(grp)
                first_write = True
                for k, ex in enumerate(grp):
                    csl = k % 2

                    def emit_cb(ex=ex, csl=csl):
                        pe_bank(6)
                        pe_bank(7)
                        W("tensor", "dv", dv_combT)

                        def fcb(t, ex=ex):
                            t.matmul(bk(6), SEL[0:32, ex * 128:(ex + 1) * 128],
                                     combT[0:32, 0:512], start=True, stop=True)
                            return t.matmul(bk(7), SEL[0:32, ex * 128:(ex + 1) * 128],
                                            combT[0:32, 512:1024], start=True, stop=True)
                        pcb = EMI("tensor", "pe", fcb)
                        W("scalar", "pe", pcb)
                        W("scalar", "dv", cbg[csl])

                        def fcbe(s, csl=csl):
                            s.copy(cb_slot[csl][:, 0:512], bk(6))
                            return s.copy(cb_slot[csl][:, 512:1024], bk(7))
                        accb = EMI("scalar", "ac", fcbe)
                        bank_guard[6] = ("ac", accb)
                        bank_guard[7] = ("ac", accb)
                        return accb

                    if not first_gate:
                        accb = emit_cb()

                    for fc in range(FC):
                        slg, cg = wr_load(mk_wquarter(i_wg, ex, fc))
                        pcg = {}
                        for ch in range(2):
                            pe_bank(ch)
                            if ch == 0:
                                W("tensor", cg[0], cg[1])
                                if first_gate:
                                    W("tensor", "dv", dvy)

                            def fg(t, slg=slg, ch=ch):
                                for dc in range(NC):
                                    mm = t.matmul(bk(ch), WR[:, slg * 2048 + dc * 128:slg * 2048 + dc * 128 + 128],
                                                  YT[:, dc * T + ch * 512:dc * T + ch * 512 + 512],
                                                  start=(dc == 0), stop=(dc == NC - 1))
                                return mm
                            pcg[ch] = EMI("tensor", "pe", fg)
                        wr_consumed(slg)
                        if first_gate:
                            first_gate = False
                            accb = emit_cb()
                        slu, cu = wr_load(mk_wquarter(i_wu, ex, fc))
                        pcu = {}
                        for ch in range(2):
                            pe_bank(2 + ch)
                            if ch == 0:
                                W("tensor", cu[0], cu[1])

                            def fu(t, slu=slu, ch=ch):
                                for dc in range(NC):
                                    mm = t.matmul(bk(2 + ch), WR[:, slu * 2048 + dc * 128:slu * 2048 + dc * 128 + 128],
                                                  YT[:, dc * T + ch * 512:dc * T + ch * 512 + 512],
                                                  start=(dc == 0), stop=(dc == NC - 1))
                                return mm
                            pcu[ch] = EMI("tensor", "pe", fu)
                        wr_consumed(slu)
                        for ch in range(2):
                            W("scalar", "pe", pcg[ch])
                            W("scalar", "dv", prev_ascr_dv)
                            acsl = EMI("scalar", "ac", lambda s, ch=ch: s.activation(
                                silscr[:, :], bk(ch), AF.Silu))
                            bank_guard[ch] = ("ac", acsl)
                            W("vector", "ac", acsl)
                            W("vector", "ac", accb)
                            if first_write:
                                W("vector", "pe", a_guard_pe)
                                first_write = False
                            W("vector", "pe", pcu[ch])
                            dv1 = EMI("vector", "dv", lambda v, k=k, fc=fc, ch=ch:
                                      v.tensor_tensor(a_sl(k, fc, ch), silscr[:, :], bk(2 + ch),
                                                      op=ALU.mult))
                            bank_guard[2 + ch] = ("dv", dv1)
                            prev_ascr_dv = dv1
                            dv2 = EMI("vector", "dv", lambda v, k=k, fc=fc, ch=ch, csl=csl:
                                      v.tensor_tensor(a_sl(k, fc, ch), a_sl(k, fc, ch),
                                                      cb_slot[csl][:, ch * 512:ch * 512 + 512],
                                                      op=ALU.mult))
                    cbg[csl] = dv2
                a_ready = CNT["dv"]
                for dc in range(NC):
                    sld, cd = wd_load(mk_wdslot(grp, dc))
                    for ch in range(2):
                        b = 4 + (dc * 2 + ch) % 2
                        pe_bank(b)
                        W("tensor", cd[0], cd[1])
                        W("tensor", "dv", a_ready)

                        def fdn(t, sld=sld, b=b, G=G, ch=ch):
                            i = 0
                            for k in range(G):
                                for fc in range(FC):
                                    o = (k * FC + fc) * 128
                                    mm = t.matmul(bk(b), wd_slot[sld][:, o:o + 128],
                                                  a_sl(k, fc, ch),
                                                  start=(i == 0), stop=(i == G * FC - 1))
                                    i += 1
                            return mm
                        pdn = EMI("tensor", "pe", fdn)
                        W("vector", "pe", pdn)
                        dve = EMI("vector", "dv", lambda v, b=b, dc=dc, ch=ch: v.tensor_tensor(
                            XT[:, dc * T + ch * 512:dc * T + ch * 512 + 512], bk(b),
                            XT[:, dc * T + ch * 512:dc * T + ch * 512 + 512], op=ALU.add))
                        bank_guard[b] = ("dv", dve)
                        dve_dc[dc] = dve
                    wd_consumed(sld)
                a_guard_pe = CNT["pe"]
            return dve_dc

        def attn_phase():
            dvy2 = ln_phase()
            W("vector", "pe", CNT["pe"])
            for b in range(2, 8):
                pe_bank(b)

            def fz(v):
                for b in range(2, 8):
                    v.memset(bk(b), 0.0)
                v.memset(BD16[:, :], 0.0)
                return v.memset(expS0, 0.0)
            dvz = EMI("vector", "dv", fz)
            for b in range(2, 8):
                bank_guard[b] = ("dv", dvz)
            scatter_last = 0
            expS_guard = 0
            PT_guard = 0
            prev_exp_ac = 0
            prev_fao2 = 0
            prev_fao2h = [0, 0]
            dvbd = 0
            vh = {}
            send_cnt = [0, 0]

            def emit_readback(q):
                W("sync", "cc", q + 1)
                W("sync", "small", WS_SMALL)
                if q == 0:
                    EM("sync", lambda sy: vh.__setitem__(
                        "v", sy.value_load(hoffs[0:1, 0:1])))
                for r in range(2):
                    def frb(sy, r=r, q=q):
                        s2 = cc_out_q[q][:, :].rearrange("(r2 j p) c -> r2 p j c",
                                                         r2=2, p=128)[r, :, :, :]
                        return sy.dma_start(
                            out=STv.rearrange("p (j q2 r2 m) -> p j q2 r2 m",
                                              q2=2, r2=2, m=256)[:, :, q, r, :],
                            in_=s2[:, :, bass.ds(vh["v"], 256)],
                        ).then_inc(sems["rb"], 16)
                    EM("sync", frb)
                    inc("rb", 16)

            def emit_collective(q):
                W("gpsimd", "send", send_cnt[q])
                EM("gpsimd", lambda g, q=q: g.collective_compute(
                    "AllGather", ALU.bypass, replica_groups=RG,
                    ins=[cc_in_q[q][:, :].opt()],
                    outs=[cc_out_q[q][:, :].opt()]).then_inc(sems["cc"], 1))
                inc("cc")
            for h2 in range(2):
                pe_h_inner_prev = CNT["pe"]
                if h2 == 1:
                    emit_readback(0)
                for mg in range(MG):
                    if h2 == 1 and mg == 3:
                        emit_collective(0)
                    slq, cq = wr_load(mk_wcol(i_wqkvT, mg))
                    qb = mg % 3
                    pe_bank(qb)
                    W("tensor", cq[0], cq[1])
                    W("tensor", "dv", dvy2)

                    def fqk(t, slq=slq, qb=qb, h2=h2):
                        for dc in range(NC):
                            mm = t.matmul(bk(qb), WR[:, slq * 2048 + dc * 128:slq * 2048 + dc * 128 + 128],
                                          YT[:, dc * T + h2 * 512:dc * T + h2 * 512 + 512],
                                          start=(dc == 0), stop=(dc == NC - 1))
                        return mm
                    pcq = EMI("tensor", "pe", fqk)
                    wr_consumed(slq)
                    W("scalar", "pe", pcq)
                    acq = EMI("scalar", "ac", lambda s, mg=mg, qb=qb: s.activation(
                        qkvT[:, mg * 512:(mg + 1) * 512], bk(qb), AF.Identity,
                        bias=bq_sb[:, mg:mg + 1], scale=1.0))
                    bank_guard[qb] = ("ac", acq)
                ac_qkv = CNT["ac"]
                for b in range(3):
                    s_, c_ = bank_guard[b]
                    W("vector", s_, c_)
                W("vector", "ac", ac_qkv)
                dvz2 = EMI("vector", "dv", lambda v: (v.memset(bk(0), 0.0), v.memset(bk(1), 0.0),
                                                      v.memset(bk(2), 0.0))[-1])
                for b in range(3):
                    bank_guard[b] = ("dv", dvz2)
                for pr in range(2):
                    tiles = (2 * pr, 2 * pr + 1)
                    sc_pe = {}
                    for ti in tiles:
                        sb = 3 + (ti % 2)
                        pe_bank(sb)
                        W("tensor", "ac", ac_qkv)

                        def fsc(t, sb=sb, ti=ti):
                            for tok in range(128):
                                u, yy = tok // 32, tok % 32
                                tau = ti * 128 + tok
                                mm = t.matmul(
                                    PS[32 * u:32 * u + 16, sb * 512 + yy * 16:sb * 512 + yy * 16 + 16],
                                    qkv3[:, 0:16, tau], qkv3[:, 16:32, tau],
                                    start=True, stop=True, tile_position=(0, 32 * u))
                            return mm
                        sc_pe[ti] = EMI("tensor", "pe", fsc)
                    for ti in tiles:
                        par = ti % 2
                        sb = 3 + par
                        W("vector", "pe", sc_pe[ti])
                        W("vector", "ac", prev_exp_ac)

                        def fsm0(v, sb=sb, par=par):
                            sc3 = bk(sb).rearrange("p (y g) -> p y g", g=16)
                            return v.reduce_max(Msc[:, par * 32:par * 32 + 32], sc3, axis=AX.X)
                        c_ = EMI("vector", "dv", fsm0)
                        W("vector", "dv", c_)

                        def fsm(v, sb=sb, par=par):
                            sc3 = bk(sb).rearrange("p (y g) -> p y g", g=16)
                            ms = Msc[:, par * 32:par * 32 + 32]
                            return v.tensor_tensor(
                                scsub.rearrange("p (y g) -> p y g", g=16), sc3,
                                ms.rearrange("p (y o) -> p y o", o=1).broadcast_to([128, 32, 16]),
                                op=ALU.subtract)
                        dvsm = EMI("vector", "dv", fsm)
                        bank_guard[sb] = ("dv", dvsm)
                        W("scalar", "dv", dvsm)
                        W("scalar", "dv", expS_guard)
                        W("scalar", "pe", PT_guard)

                        def fex(s):
                            return s.activation(
                                expS0.rearrange("p (y g) -> p y g", g=32)[:, :, 0:16],
                                scsub.rearrange("p (y g) -> p y g", g=16), AF.Exp, scale=ISQ)
                        ace = EMI("scalar", "ac", fex)
                        prev_exp_ac = ace
                        W("vector", "ac", ace)

                        def fpt0(v, par=par):
                            e3 = expS0.rearrange("p (y g) -> p y g", g=32)[:, :, 0:16]
                            return v.reduce_sum(Msc[:, par * 32:par * 32 + 32], e3, axis=AX.X)
                        c_ = EMI("vector", "dv", fpt0)
                        W("vector", "dv", c_)

                        def fpt1(v, par=par):
                            return v.reciprocal(Rsb[:, par * 32:par * 32 + 32],
                                                Msc[:, par * 32:par * 32 + 32])
                        c_ = EMI("vector", "dv", fpt1)
                        W("vector", "dv", c_)

                        def fscale(v, par=par):
                            e3 = expS0.rearrange("p (y g) -> p y g", g=32)[:, :, 0:16]
                            rr = Rsb[:, par * 32:par * 32 + 32].rearrange(
                                "p (y o) -> p y o", o=1).broadcast_to([128, 32, 16])
                            return v.tensor_tensor(e3, e3, rr, op=ALU.mult)
                        c_ = EMI("vector", "dv", fscale)
                        W("vector", "dv", c_)

                        def fpt(v, par=par):
                            return v.transpose(PTv0, expS0)
                        ptdv = EMI("vector", "dv", fpt)
                        expS_guard = ptdv
                        W("tensor", "dv", ptdv)
                        for bi in range(4):
                            gy, jh = bi // 2, bi % 2
                            bp = bi % 2
                            vb = [5, 0][bp]
                            ob = [6, 1][bp]
                            ys = [gy * 16 + jh * 8 + k for k in range(8)]
                            if jh == 0:
                                W("vector", "pe", prev_fao2)

                                def fbd(v, gy=gy):
                                    for u in range(4):
                                        src = PTv0[32 * u:32 * u + 16, :].rearrange(
                                            "p (y h) -> p y h", h=32)[:, gy * 16:(gy + 1) * 16, 0:16]
                                        dst = BD16[32 * u:32 * u + 16, :].rearrange(
                                            "p (y c) -> p y c", c=64)[:, :, u * 16:u * 16 + 16]
                                        mm = v.tensor_copy(dst, src)
                                    return mm
                                dvbd = EMI("vector", "dv", fbd)
                            pe_bank(vb)

                            def fvtb(t, vb=vb, ti=ti, ys=ys):
                                for k, y in enumerate(ys):
                                    for u in range(4):
                                        tau = ti * 128 + u * 32 + y
                                        mm = t.transpose(
                                            bkb16(vb)[32 * u:32 * u + 16, k * 128:(k + 1) * 128],
                                            qkv3[:, 32:48, tau], identB[:, :],
                                            tile_position=(0, 32 * u))
                                return mm
                            pv = EMI("tensor", "pe", fvtb)
                            W("vector", "pe", pv)
                            W("vector", "pe", prev_fao2)
                            dv1 = EMI("vector", "dv", lambda v, vb=vb: v.tensor_copy(
                                v4b[:, :], bkb16(vb)[:, 0:1024]))
                            bank_guard[vb] = ("dv", dv1)
                            pe_bank(ob)
                            W("tensor", "dv", dv1)
                            W("tensor", "dv", dvbd)

                            def fao2(t, ob=ob, ys=ys):
                                for k, y in enumerate(ys):
                                    yl = y % 16
                                    mm = t.matmul(bk(ob)[:, k * 64:(k + 1) * 64],
                                                  v4b[:, k * 128:(k + 1) * 128],
                                                  BD16[:, yl * 64:(yl + 1) * 64],
                                                  start=True, stop=True)
                                return mm
                            pf = EMI("tensor", "pe", fao2)
                            prev_fao2 = pf
                            PT_guard = pf
                            W("vector", "pe", pf)

                            def fsctb(v, ob=ob, ti=ti, gy=gy, jh=jh, h2=h2):
                                src4 = bk(ob).rearrange("p (ys2 u h) -> p ys2 u h",
                                                        ys2=8, u=4, h=16)
                                dstv = sendv.rearrange("p (j c) -> p j c",
                                                       c=1024)[:, jh * 8:jh * 8 + 8, :]
                                dstl = dstv.rearrange("p j (l g) -> p j l g", g=32)
                                for u in range(4):
                                    mm = v.tensor_copy(
                                        dstl[:, :, h2 * 16:(h2 + 1) * 16, ti * 8 + u * 2 + gy],
                                        src4[:, :, u, :])
                                return mm
                            dv3 = EMI("vector", "dv", fsctb)
                            bank_guard[ob] = ("dv", dv3)
                            scatter_last = dv3
                W("sync", "dv", scatter_last)

                def fsend(sy, q=h2):
                    return sy.dma_start(
                        out=cc_in_q[q][:, :].rearrange("(j p) c -> p j c", p=128),
                        in_=sendv.rearrange("p (j q2 c) -> p j q2 c",
                                            q2=2, c=512)[:, :, q, :],
                    ).then_inc(sems["send"], 16)
                EM("sync", fsend)
                send_cnt[h2] = inc("send", 16)
            emit_readback(1)
            for q in range(2):
                W("tensor", "rb", 32 * (q + 1))
                for mg in range(NC):
                    if q == 0 and mg == 3:
                        emit_collective(1)
                    slw, cw = wr_load(mk_wcol(i_woutT, mg))
                    b = 2 * q + (mg % 2)
                    pe_bank(b)
                    W("tensor", cw[0], cw[1])

                    def fpj(t, slw=slw, b=b, mg=mg, q=q):
                        for j in range(NC):
                            mm = t.matmul(bk(b), WR[:, slw * 2048 + j * 128:slw * 2048 + j * 128 + 128],
                                          STv[:, j * 1024 + q * 512:j * 1024 + q * 512 + 512],
                                          start=(j == 0), stop=(j == NC - 1))
                        return mm
                    ppj = EMI("tensor", "pe", fpj)
                    W("vector", "pe", ppj)

                    def fpe(v, b=b, mg=mg, q=q):
                        xr = XT[:, mg * T:(mg + 1) * T].rearrange(
                            "p (hp r2 q2 g) -> p hp r2 q2 g", hp=8, r2=2, q2=2, g=32)
                        for r in range(2):
                            xv = xr[:, :, r, q, :]
                            bkv = bk(b)[:, r * 256:(r + 1) * 256].rearrange(
                                "p (hp g) -> p hp g", g=32)
                            mm = v.scalar_tensor_tensor(
                                xv, bkv, bo_sb[:, mg:mg + 1], xv,
                                op0=ALU.add, op1=ALU.add)
                        return mm
                    dvp = EMI("vector", "dv", fpe)
                    bank_guard[b] = ("dv", dvp)
                    wr_consumed(slw)
            return CNT["dv"]

        if debug:
            W("sync", "dv", CNT["dv"])
            EM("sync", lambda sy: sy.dma_start(out=dbg["d_x0"], in_=XT[:, :]).then_inc(sems["send"], 16))
            inc("send", 16)
        dv_xt0 = CNT["dv"]
        dvy1_ = ln_phase()
        dvc = router_phase(dv_xt0)
        if debug:
            W("sync", "dv", max(dvy1_, dvc))
            EM("sync", lambda sy: sy.dma_start(out=dbg["d_y1"], in_=YT[:, :]).then_inc(sems["send"], 16))
            EM("sync", lambda sy: sy.dma_start(out=dbg["d_comb"], in_=combT[0:32, :]).then_inc(sems["send"], 16))
            EM("sync", lambda sy: sy.dma_start(out=dbg["d_rt"][0:18, :], in_=rt_sb[0:18, :]).then_inc(sems["send"], 16))
            EM("sync", lambda sy: sy.dma_start(out=dbg["d_l17"], in_=l17a[:, 0:256]).then_inc(sems["send"], 16))
            EM("sync", lambda sy: sy.dma_start(out=dbg["d_m8"], in_=m8a[:, 0:64]).then_inc(sems["send"], 16))
            EM("sync", lambda sy: sy.dma_start(out=dbg["d_mask"], in_=maska[:, 0:128]).then_inc(sems["send"], 16))
            EM("sync", lambda sy: sy.dma_start(out=dbg["d_me"], in_=expva[:, 0:128]).then_inc(sems["send"], 16))
            EM("sync", lambda sy: sy.dma_start(out=dbg["d_c17"], in_=c17a[:, 0:256]).then_inc(sems["send"], 16))
            EM("sync", lambda sy: sy.dma_start(out=dbg["d_nmv"], in_=nmsnapv[:, :]).then_inc(sems["send"], 16))
            EM("sync", lambda sy: sy.dma_start(out=dbg["d_nma"], in_=nmsnapa[:, :]).then_inc(sems["send"], 16))
            inc("send", 160)
        dv1 = moe_phase(dvc, 0, dvy1_)
        if debug:
            W("sync", "dv", max(dv1.values()))
            EM("sync", lambda sy: sy.dma_start(out=dbg["d_x1"], in_=XT[:, :]).then_inc(sems["send"], 16))
            inc("send", 16)
        dv2 = attn_phase()
        if debug:
            W("sync", "dv", dv2)
            EM("sync", lambda sy: sy.dma_start(out=dbg["d_x2"], in_=XT[:, :]).then_inc(sems["send"], 16))
            inc("send", 16)
        dv_xt1 = CNT["dv"]
        dvy3_ = ln_phase()
        dvc3 = router_phase(dv_xt1)
        wd_state["guard"] = [CNT["pe"], CNT["pe"]]
        dv3 = moe_phase(dvc3, 0, dvy3_)

        for dc in range(NC):
            W("sync", "dv", dv3[dc])
            EM("sync", lambda sy, dc=dc: sy.dma_start(
                out=o_y.rearrange("(dc2 p) t -> p dc2 t", p=128)[:, dc, :],
                in_=XT[:, dc * T:(dc + 1) * T]).then_inc(sems["xs1"], 16))
            inc("xs1", 16)
        W("sync", "xs1", CNT["xs1"])
        W("sync", "send", CNT["send"])

        def runner(name):
            def run(engine):
                for item in Q[name]:
                    if isinstance(item, tuple) and item[0] == "W":
                        engine.wait_ge(sems[item[1]], item[2])
                    else:
                        item(engine)
            return run

        @blk.tensor
        def _(t):
            runner("tensor")(t)

        @blk.vector
        def _(v):
            runner("vector")(v)

        @blk.scalar
        def _(s):
            runner("scalar")(s)

        @blk.gpsimd
        def _(g):
            runner("gpsimd")(g)

        @blk.sync
        def _(sy):
            runner("sync")(sy)

    return nc


def _prep(inputs):
    g = lambda k: np.ascontiguousarray(np.asarray(inputs[k], dtype=np.float32))
    x = g("L0_attn_x")
    wqkvT = np.ascontiguousarray(
        g("L0_attn_w_qkv").reshape(48, 128, 16, 128).transpose(0, 3, 2, 1).reshape(48, 128, 2048))
    woutT = np.ascontiguousarray(
        g("L0_attn_w_out").reshape(16, 128, 16, 128).transpose(0, 3, 2, 1).reshape(16, 128, 2048))
    g17 = np.concatenate([g("L0_ffn_gate_w").T, g("L0_ffn_shared_gate_w"),
                          np.zeros((2048, 1), np.float32)], axis=1)
    g17 = np.ascontiguousarray(g17.reshape(16, 128, 18).transpose(1, 0, 2))
    bq = np.ascontiguousarray(g("L0_attn_b_qkv").reshape(48, 128).T)
    bo = np.ascontiguousarray(g("L0_attn_b_out").reshape(16, 128).T)
    wg0 = np.concatenate([g("L0_ffn_Wg"), g("L0_ffn_sWg")[None]], axis=0)
    wu0 = np.concatenate([g("L0_ffn_Wu"), g("L0_ffn_sWu")[None]], axis=0)
    wd0 = np.concatenate([g("L0_ffn_Wd"), g("L0_ffn_sWd")[None]], axis=0)
    wg = np.ascontiguousarray(
        wg0.reshape(17, 16, 128, 4, 128).transpose(0, 3, 2, 1, 4).reshape(17, 4, 128, 2048))
    wu = np.ascontiguousarray(
        wu0.reshape(17, 16, 128, 4, 128).transpose(0, 3, 2, 1, 4).reshape(17, 4, 128, 2048))
    wd = np.ascontiguousarray(
        wd0.reshape(17, 4, 128, 16, 128).transpose(3, 2, 0, 1, 4).reshape(16, 128, 17 * 512))
    return x, wqkvT, woutT, g17, bq, bo, wg, wu, wd


def kernel(**inputs):
    x, wqkvT, woutT, g17, bq, bo, wg, wu, wd = _prep(inputs)
    nc = build_nc(False)
    in_maps = []
    for c in range(8):
        b, hf = c // 2, c % 2
        in_maps.append({
            "i_x": np.ascontiguousarray(x[b, hf * 1024:(hf + 1) * 1024, :].T),
            "i_wqkvT": wqkvT, "i_woutT": woutT, "i_g17": g17,
            "i_bq": bq, "i_bo": bo, "i_wg": wg, "i_wu": wu, "i_wd": wd,
            "i_hoff": np.array([[hf * 256]], dtype=np.int32),
        })
    res = run_bass_kernel_spmd(nc, in_maps, core_ids=list(range(8)))
    out = np.empty((4, 2048, 2048), np.float32)
    for c in range(8):
        out[c // 2, (c % 2) * 1024:(c % 2 + 1) * 1024, :] = np.asarray(res.results[c]["o_y"]).T
    return out
